# version 22
# baseline (speedup 1.0000x reference)
import numpy as np
import ml_dtypes
import concourse.bass as bass
import concourse.mybir as mybir
from concourse.bass_utils import run_bass_kernel_spmd
from contextlib import ExitStack

F32 = mybir.dt.float32
BF16 = mybir.dt.bfloat16
U8 = mybir.dt.uint8
AF = mybir.ActivationFunctionType
ALU = mybir.AluOpType
AX = mybir.AxisListType

P = 128
D = 4096
NT = 8
TOK = NT * P
NC = D // P
NH = 32
DQ = 128
SCALE = 1.0 / float(np.sqrt(128.0))
EPS = 1e-5
NQ8 = 8
DS = DQ // NQ8

_V_OPS = {"bn_stats", "bn_aggr",
          "tensor_tensor", "tensor_add", "tensor_sub", "tensor_mul", "tensor_max",
          "tensor_copy", "tensor_scalar", "tensor_single_scalar", "tensor_scalar_mul",
          "tensor_scalar_add", "scalar_tensor_tensor", "tensor_reduce", "reduce_sum",
          "reduce_max", "reciprocal", "copy_predicated", "select", "memset", "memzero"}
_A_OPS = {"activation", "copy", "add", "mul", "square", "sqrt", "memzero"}


class _DryIns:
    def then_inc(self, *a, **k):
        return self


class _Dry:
    def __getattr__(self, name):
        def f(*a, **kw):
            return _DryIns()
        return f


class Prox:
    def __init__(self, eng, sem, ops):
        self._eng = eng
        self._sem = sem
        self._ops = ops
        self.n = 0

    def __getattr__(self, name):
        fn = getattr(self._eng, name)
        if name in self._ops:
            def wrapped(*a, **kw):
                ins = fn(*a, **kw)
                ins.then_inc(self._sem, 1)
                self.n += 1
                return ins
            return wrapped
        return fn

    def wait_prev(self):
        if self.n > 0:
            self._eng.wait_ge(self._sem, self.n)


def build_nc():
    nc = bass.Bass("TRN2", target_bir_lowering=False)

    x_in = nc.dram_tensor("x", [TOK, D], F32, kind="ExternalInput").ap()
    xT_in = nc.dram_tensor("xT", [D, TOK], F32, kind="ExternalInput").ap()
    wq_in = nc.dram_tensor("wq", [512, D], F32, kind="ExternalInput").ap()
    wqsl_d = nc.dram_tensor("wqsl", [P, 512], BF16, kind="Internal").ap()
    wqsg_d = nc.dram_tensor("wqsg", [8 * P, 512], BF16, kind="Internal").ap()
    bq_in = nc.dram_tensor("bq", [DQ, NH], F32, kind="ExternalInput").ap()
    wk_in = nc.dram_tensor("wk", [D, DQ], F32, kind="ExternalInput").ap()
    bk_in = nc.dram_tensor("bk", [DQ, 1], F32, kind="ExternalInput").ap()
    wv_in = nc.dram_tensor("wv", [D, DQ], F32, kind="ExternalInput").ap()
    bv_in = nc.dram_tensor("bv", [DQ, 1], F32, kind="ExternalInput").ap()
    w0th_in = nc.dram_tensor("w0th", [NC, P, D], F32, kind="ExternalInput").ap()
    b0f_in = nc.dram_tensor("b0f", [P, NC], F32, kind="ExternalInput").ap()
    at_in = nc.dram_tensor("at", [D, P], F32, kind="ExternalInput").ap()
    bt_in = nc.dram_tensor("bt", [P, D], F32, kind="ExternalInput").ap()
    mt_in = [nc.dram_tensor(f"mt{l}", [D, TOK], U8, kind="ExternalInput").ap() for l in (1, 2, 3)]
    selt_in = nc.dram_tensor("selt", [P, D], BF16, kind="ExternalInput").ap()
    idf_in = nc.dram_tensor("idf", [P, P], F32, kind="ExternalInput").ap()
    idb_in = nc.dram_tensor("idb", [P, P], BF16, kind="ExternalInput").ap()
    cvec_in = nc.dram_tensor("cvec", [P, 8], F32, kind="ExternalInput").ap()
    out_d = nc.dram_tensor("out", [TOK, D], F32, kind="ExternalOutput").ap()
    abounce = nc.dram_tensor("abounce", [NT, P, P], BF16, kind="Internal").ap()
    kbounce = nc.dram_tensor("kbounce", [NT, P, P], BF16, kind="Internal").ap()

    with ExitStack() as ctx:
        def sb(name, shape, dt):
            return ctx.enter_context(nc.sbuf_tensor(name, shape, dt))[:]

        idf = sb("idf_s", [P, P], F32)
        idb = sb("idb_s", [P, P], BF16)
        selt = sb("selt_s", [P, D], BF16)
        wqs = sb("wqs", [P, D], BF16)
        wk = sb("wk_s", [P, D], BF16)
        wv = sb("wv_s", [P, D], BF16)
        at_pad = sb("at_pad", [P, D], BF16)
        bt_pad = sb("bt_pad", [P, D], BF16)
        b0f = sb("b0f_s", [P, NC], F32)
        bq_sb = sb("bq_sb", [DQ, NH], F32)
        bkv = sb("bkv", [DQ, 2], F32)
        stats = sb("stats", [P, 64], F32)
        bqs = sb("bqs_c", [P, 2], F32)
        cvec = sb("cvec_s", [P, 8], F32)
        xnT = sb("xnT", [P, NC * TOK], BF16)
        hT = sb("hT", [P, TOK], BF16)
        o_fm = sb("o_fm", [P, TOK], BF16)
        a_tm = sb("a_tm", [P, TOK], BF16)
        k_tm = sb("k_tm", [P, TOK], BF16)
        v1 = sb("v1", [P, 2 * TOK], BF16)
        rzb = sb("rzb", [P, 256], F32)

        AR1 = sb("AR1", [P, 32768], U8)
        AR2 = sb("AR2", [P, 32768], U8)
        AR3 = sb("AR3", [P, 16384], U8)

        def view(ar, off, shape, dt):
            v = ar[:shape[0], off:off + int(np.prod(shape[1:])) * mybir.dt.size(dt)].bitcast(dt)
            if len(shape) == 3:
                v = v.rearrange("p (a b) -> p a b", a=shape[1])
            return v

        xload = [view(AR1, 0, [P, D], F32), view(AR1, 16384, [P, D], F32)]
        junkb = view(AR3, 0, [P, D], BF16)
        xn_buf = view(AR3, 8192, [P, D], BF16)
        wqc = [view(AR2, b * 8192, [P, 2048], F32) for b in range(4)]
        aT_f = view(AR1, 0, [P, TOK], F32)
        kT_f = view(AR1, 4096, [P, TOK], F32)
        vT_f = view(AR1, 8192, [P, TOK], F32)
        Ebuf = [view(AR3, 0, [P, TOK], BF16), view(AR3, 2048, [P, TOK], BF16)]
        numZ = view(AR3, 4096, [P, 2048], F32)
        afm = [AR1[0:1, 12288 + b * 4096:12288 + (b + 1) * 4096].bitcast(BF16) for b in range(2)]
        kfm = [AR1[0:1, 20480 + b * 4096:20480 + (b + 1) * 4096].bitcast(BF16) for b in range(2)]
        xTb = [view(AR1, 0, [P, TOK], F32), view(AR1, 4096, [P, TOK], F32)]
        rest = [view(AR1, 8192, [P, TOK], F32), view(AR1, 12288, [P, TOK], F32)]
        outst = [view(AR1, 16384, [P, TOK], F32), view(AR1, 20480, [P, TOK], F32)]
        base1 = [view(AR1, 24576, [P, 512], F32), view(AR1, 26624, [P, 512], F32)]
        selb = view(AR1, 28672, [P, 512], F32)
        wch = [view(AR2, 0, [P, D], BF16), view(AR2, 8192, [P, D], BF16)]
        mtb = [[view(AR2, 24576 + (pl * 2 + b) * 1024, [P, TOK], U8) for b in range(2)]
               for pl in range(3)]
        bnst = sb("bnst", [P, 48], F32)
        xnb = [xn_buf, junkb]

        PS = ctx.enter_context(nc.psum_tensor("PS", [P, 4096], F32))[:]
        PSb = PS.bitcast(BF16)
        bank = lambda b: PS[:, 512 * b:512 * (b + 1)]
        bankb = lambda b: PSb[:, 1024 * b:1024 * (b + 1)]

        sems = {}
        for name in ["s_vp", "s_ap",
                     "s_xa0", "s_xa1", "s_wqc0", "s_wqc1", "s_wqc2", "s_wqc3",
                     "s_cst", "s_gcst", "s_tr", "s_sel",
                     "s_ptA", "s_qpe", "s_akvp",
                     "s_ad", "s_so", "s_ec", "s_afm0", "s_afm1",
                     "s_wql", "s_ccv", "s_wqg",
                     "s_wch0", "s_wch1", "s_mt0", "s_mt1", "s_xt0", "s_xt1",
                     "s_peb", "s_pey", "s_ptE", "s_out"]:
            sems[name] = ctx.enter_context(nc.semaphore(name))
        S = type("SS", (), sems)
        s_wqc = [S.s_wqc0, S.s_wqc1, S.s_wqc2, S.s_wqc3]
        s_xa = [S.s_xa0, S.s_xa1]
        s_afm = [S.s_afm0, S.s_afm1]
        s_wch = [S.s_wch0, S.s_wch1]
        s_mt = [S.s_mt0, S.s_mt1]
        s_xt = [S.s_xt0, S.s_xt1]

        marks = {}
        cT = lambda T, i: stats[:, 8 * T + i:8 * T + i + 1]

        def make_pass(counting):
            def mark(name, px):
                if counting:
                    marks[name] = px.n
                else:
                    assert marks[name] == px.n, (name, marks[name], px.n)

            def waitm(eng, name, sem):
                if not counting:
                    eng.wait_ge(sem, marks[name])

            def wq_tree(vector, k2):
                vector.wait_ge(s_wqc[k2 % 4], 16 * (k2 // 4 + 1))
                v32 = wqc[k2 % 4].rearrange("p (d h) -> p d h", d=64)
                k, h = k2 // 2, k2 % 2
                with nc.allow_low_precision(reason="head-sum output feeds bf16 matmul"):
                    vector.reduce_sum(wqs[:, k * 128 + h * 64:k * 128 + (h + 1) * 64],
                                      v32, axis=AX.X)
                mark(f"tree{k2}", vector)

            def xnT_evac(scalar, T):
                for g in range(4):
                    i = 4 * T + g
                    scalar.wait_ge(S.s_ptA, i + 1)
                    dst = xnT.rearrange("p (c t) -> p c t", c=NC)[:, g * 8:(g + 1) * 8,
                                                                 T * P:(T + 1) * P]
                    src = bankb(4 + g).rearrange("p (c t) -> p c t", c=8)
                    scalar.activation(dst, src, AF.Copy)
                    mark(f"xne{i}", scalar)

            def emit_vector(vector):
                for T in range(NT):
                    vector.wait_ge(s_xa[T % 2], 16 * (T // 2 + 1))
                    for g in range(8):
                        vector.bn_stats(bnst[:, 6 * g:6 * (g + 1)],
                                        xload[T % 2][:, 512 * g:512 * (g + 1)])
                    vector.wait_prev()
                    vector.bn_aggr(stats[:, 8 * T + 2:8 * T + 4], bnst)
                    mark(f"lnd{T}", vector)
                    waitm(vector, f"rst{T}", S.s_ap)
                    vector.wait_prev()
                    vector.scalar_tensor_tensor(cT(T, 6), cT(T, 2), -1.0, cT(T, 5),
                                                ALU.mult, ALU.mult)
                    mark(f"bia{T}", vector)
                    if T < 4:
                        wq_tree(vector, 2 * T)
                        wq_tree(vector, 2 * T + 1)
                vector.wait_ge(S.s_cst, 16 * 7)
                vector.reduce_sum(bqs[:, 1:2], bq_sb, axis=AX.X)
                vector.wait_prev()
                vector.tensor_scalar_mul(bqs[:, 0:1], bqs[:, 1:2], float(SCALE))
                mark("bqs", vector)
                vector.wait_ge(S.s_qpe, 4)
                vector.tensor_scalar_mul(hT, PS[:, 3072:4096], 8.0)
                mark("he", vector)
                for T in range(NT):
                    vector.wait_ge(S.s_akvp, T + 1)
                    vector.tensor_copy(a_tm[:, T * P:(T + 1) * P], PS[:, 3584:3712])
                    vector.tensor_copy(k_tm[:, T * P:(T + 1) * P], PS[:, 3712:3840])
                    mark(f"akve{T}", vector)
                vector.memset(v1, 1.0)
                vector.wait_prev()
                vector.tensor_copy(
                    v1.rearrange("p (t two) -> p t two", two=2)[:, :, 0:1],
                    vT_f[:, :, None])
                mark("v1m", vector)
                for b in range(4):
                    vector.wait_ge(S.s_ec, 32 * (b + 1))
                    vector.tensor_copy(numZ[:, b * 512:(b + 1) * 512], bank(4 + b))
                    nzv = numZ[:, b * 512:(b + 1) * 512].rearrange(
                        "p (t two) -> p t two", two=2)
                    vector.wait_prev()
                    vector.reciprocal(rzb, nzv[:, :, 1:2].rearrange("p t two -> p (t two)"))
                    vector.wait_prev()
                    vector.tensor_tensor(o_fm[:, b * 256:(b + 1) * 256],
                                         nzv[:, :, 0:1].rearrange("p t two -> p (t two)"),
                                         rzb, ALU.mult)
                    mark(f"ofm{b}", vector)
                for og in range(NC):
                    for th in range(2):
                        j = 2 * og + th
                        vector.wait_ge(S.s_pey, j + 1)
                        vector.wait_ge(s_mt[og % 2], 48 * (og // 2 + 1))
                        vector.tensor_copy(selb, bank(2))
                        for pl in range(3):
                            vector.wait_prev()
                            vector.copy_predicated(
                                selb, mtb[pl][og % 2][:, th * 512:(th + 1) * 512],
                                bank(3 + pl))
                        mark(f"sel{j}", vector)
                        waitm(vector, f"b1{j}", S.s_ap)
                        vector.wait_ge(s_xt[og % 2], 16 * (og // 2 + 1))
                        if og >= 2 and th == 0:
                            vector.wait_ge(S.s_ptE, 2 * (og - 1))
                        rst = rest[og % 2][:, th * 512:(th + 1) * 512]
                        vector.wait_prev()
                        vector.scalar_tensor_tensor(rst, base1[j % 2], 1.0, selb,
                                                    ALU.mult, ALU.add)
                        vector.wait_prev()
                        vector.scalar_tensor_tensor(
                            rst, xTb[og % 2][:, th * 512:(th + 1) * 512], 1.0, rst,
                            ALU.mult, ALU.add)
                        mark(f"cmb{j}", vector)

            def emit_scalar(scalar):
                for T in range(NT):
                    if T >= 1:
                        xnT_evac(scalar, T - 1)
                    if T == 0:
                        scalar.wait_ge(S.s_cst, 16 * 7)
                    waitm(scalar, f"lnd{T}", S.s_vp)
                    scalar.activation(cT(T, 4), cT(T, 3), AF.Ln, bias=cvec[:, 0:1])
                    scalar.wait_prev()
                    scalar.activation(cT(T, 5), cT(T, 4), AF.Exp, scale=cvec[:, 1:2])
                    mark(f"rst{T}", scalar)
                    waitm(scalar, f"bia{T}", S.s_vp)
                    if T >= 2:
                        scalar.wait_ge(S.s_ptA, 4 * (T - 1))
                    scalar.wait_prev()
                    scalar.activation(xnb[T % 2], xload[T % 2], AF.Identity,
                                      bias=cT(T, 6), scale=cT(T, 5))
                    mark(f"xn{T}", scalar)
                xnT_evac(scalar, NT - 1)
                scalar.wait_ge(S.s_qpe, 1)
                waitm(scalar, "bqs", S.s_vp)
                scalar.activation(aT_f, PS[:, 0:1024], AF.Identity,
                                  bias=bqs[:, 0:1], scale=cvec[:, 2:3])
                mark("qke1", scalar)
                scalar.wait_ge(S.s_qpe, 2)
                scalar.activation(kT_f, PS[:, 1024:2048], AF.Identity, bias=bkv[:, 0:1])
                mark("qke2", scalar)
                scalar.wait_ge(S.s_qpe, 3)
                scalar.activation(vT_f, PS[:, 2048:3072], AF.Identity, bias=bkv[:, 1:2])
                mark("qke3", scalar)
                for g in range(2 * NT * NQ8):
                    scalar.wait_ge(S.s_so, g + 1)
                    if g >= 2:
                        scalar.wait_ge(S.s_ec, g - 1)
                    scalar.activation(Ebuf[g % 2], PS[:, 1024 * (g % 2):1024 * (g % 2) + 1024],
                                      AF.Exp)
                    mark(f"Eg{g}", scalar)
                for og in range(NC):
                    for th in range(2):
                        j = 2 * og + th
                        scalar.wait_ge(S.s_peb, j + 1)
                        if j >= 2:
                            waitm(scalar, f"cmb{j - 2}", S.s_vp)
                        scalar.activation(base1[j % 2], bank(th),
                                          AF.Identity, bias=b0f[:, og:og + 1])
                        mark(f"b1{j}", scalar)
                    if og >= 1:
                        scalar.wait_ge(S.s_ptE, 2 * og)
                        if og >= 2:
                            scalar.wait_ge(S.s_out, 16 * (og - 1))
                        scalar.activation(outst[(og - 1) % 2][:, 0:512], bank(6), AF.Copy)
                        scalar.wait_prev()
                        scalar.activation(outst[(og - 1) % 2][:, 512:1024], bank(7), AF.Copy)
                        mark(f"ost{og - 1}", scalar)
                scalar.wait_ge(S.s_ptE, 2 * NC)
                scalar.wait_ge(S.s_out, 16 * (NC - 1))
                scalar.activation(outst[(NC - 1) % 2][:, 0:512], bank(6), AF.Copy)
                scalar.wait_prev()
                scalar.activation(outst[(NC - 1) % 2][:, 512:1024], bank(7), AF.Copy)
                mark(f"ost{NC - 1}", scalar)

            def emit_sync(sync):
                sync.dma_start(out=cvec, in_=cvec_in).then_inc(S.s_cst, 16)
                sync.dma_start(out=xload[0],
                               in_=x_in[0:P, :]).then_inc(s_xa[0], 16)
                sync.dma_start(out=idf, in_=idf_in).then_inc(S.s_cst, 16)
                sync.dma_start(out=idb, in_=idb_in).then_inc(S.s_cst, 16)
                sync.wait_ge(s_xa[0], 16)
                sync.dma_start(out=xload[1],
                               in_=x_in[P:2 * P, :]).then_inc(s_xa[1], 16)
                sync.dma_start(out=bq_sb, in_=bq_in).then_inc(S.s_cst, 16)
                sync.dma_start(out=bkv[:, 0:1], in_=bk_in).then_inc(S.s_cst, 16)
                sync.dma_start(out=bkv[:, 1:2], in_=bv_in).then_inc(S.s_cst, 16)
                sync.dma_start(out=b0f, in_=b0f_in).then_inc(S.s_cst, 16)
                def wq_half(k2):
                    k, h = k2 // 2, k2 % 2
                    if k2 >= 4:
                        waitm(sync, f"tree{k2 - 4}", S.s_vp)
                    sync.dma_start(
                        out=wqc[k2 % 4],
                        in_=wq_in[k * P:(k + 1) * P, h * 2048:(h + 1) * 2048],
                    ).then_inc(s_wqc[k2 % 4], 16)
                for k2 in range(2):
                    wq_half(k2)
                sync.dma_start(out=selt, in_=selt_in).then_inc(S.s_sel, 16)
                for k2 in range(2, 4):
                    wq_half(k2)
                for k in range(2, NT):
                    waitm(sync, f"xn{k - 2}", S.s_ap)
                    sync.dma_start(out=xload[k % 2],
                                   in_=x_in[k * P:(k + 1) * P, :]).then_inc(s_xa[k % 2], 16)
                    if k in (2, 3):
                        wq_half(2 * k)
                        wq_half(2 * k + 1)
                sync.wait_ge(S.s_ccv, 1)
                sync.dma_start(
                    out=wqs.rearrange("p (r j d) -> p r j d", r=8, j=4),
                    in_=wqsg_d.rearrange("(r p) (j d) -> p r j d", p=P, j=4),
                ).then_inc(S.s_wqg, 16)
                for c32 in range(64):
                    T = c32 // 8
                    sync.wait_ge(S.s_ad, 32 * (T + 1))
                    if c32 >= 2:
                        sync.wait_ge(S.s_so, 2 * (c32 - 1))
                    off = (c32 % 8) * 2048
                    sync.dma_start(out=afm[c32 % 2],
                                   in_=abounce[T].rearrange("t d -> (t d)")[off:off + 2048][None, :],
                                   ).then_inc(s_afm[c32 % 2], 16)
                    sync.dma_start(out=kfm[c32 % 2],
                                   in_=kbounce[T].rearrange("t d -> (t d)")[off:off + 2048][None, :],
                                   ).then_inc(s_afm[c32 % 2], 16)
                for og in range(NC):
                    if og == 0:
                        waitm(sync, "tree7", S.s_vp)
                    if og >= 2:
                        waitm(sync, f"sel{2 * (og - 2) + 1}", S.s_vp)
                    for pl in range(3):
                        sync.dma_start(out=mtb[pl][og % 2],
                                       in_=mt_in[pl][og * P:(og + 1) * P, :]).then_inc(s_mt[og % 2], 16)
                    if og == 0:
                        sync.wait_ge(S.s_akvp, NT)
                        waitm(sync, "v1m", S.s_vp)
                    if og >= 2:
                        waitm(sync, f"cmb{2 * (og - 2) + 1}", S.s_vp)
                    sync.dma_start(out=xTb[og % 2],
                                   in_=xT_in[og * P:(og + 1) * P, :]).then_inc(s_xt[og % 2], 16)
                    if og >= 1:
                        waitm(sync, f"ost{og - 1}", S.s_ap)
                        sync.dma_start(
                            out=out_d[:, (og - 1) * P:og * P].rearrange("(T t) j -> t T j", t=P),
                            in_=outst[(og - 1) % 2].rearrange("p (T j) -> p T j", T=NT),
                        ).then_inc(S.s_out, 16)
                waitm(sync, f"ost{NC - 1}", S.s_ap)
                sync.dma_start(
                    out=out_d[:, (NC - 1) * P:NC * P].rearrange("(T t) j -> t T j", t=P),
                    in_=outst[(NC - 1) % 2].rearrange("p (T j) -> p T j", T=NT),
                ).then_inc(S.s_out, 16)
                sync.wait_ge(S.s_out, 16 * NC)

            def emit_gpsimd(gpsimd):
                waitm(gpsimd, "tree7", S.s_vp)
                gpsimd.dma_start(out=wqsl_d, in_=wqs[:, 0:512]).then_inc(S.s_wql, 16)
                gpsimd.wait_ge(S.s_wql, 16)
                gpsimd.collective_compute(
                    "AllGather", ALU.bypass,
                    replica_groups=[list(range(8))],
                    ins=[wqsl_d], outs=[wqsg_d]).then_inc(S.s_ccv)
                gpsimd.dma_start(out=wk, in_=wk_in.rearrange("(c p) d -> p c d", p=P),
                                 ).then_inc(S.s_gcst, 16)
                gpsimd.dma_start(out=wv, in_=wv_in.rearrange("(c p) d -> p c d", p=P),
                                 ).then_inc(S.s_gcst, 16)
                gpsimd.dma_start(out=at_pad, in_=at_in.rearrange("(c p) q -> p c q", p=P),
                                 ).then_inc(S.s_gcst, 16)
                gpsimd.dma_start(out=bt_pad, in_=bt_in).then_inc(S.s_gcst, 16)
                for T in range(NT):
                    waitm(gpsimd, f"akve{T}", S.s_vp)
                    if T >= 1:
                        gpsimd.wait_ge(S.s_ad, 32 * T)
                    gpsimd.dma_start(out=abounce[T],
                                     in_=a_tm[:, T * P:(T + 1) * P]).then_inc(S.s_ad, 16)
                    gpsimd.dma_start(out=kbounce[T],
                                     in_=k_tm[:, T * P:(T + 1) * P]).then_inc(S.s_ad, 16)
                for og in range(NC):
                    if og == 0:
                        waitm(gpsimd, "tree7", S.s_vp)
                    if og >= 2:
                        gpsimd.wait_ge(S.s_peb, 2 * (og - 1))
                    gpsimd.dma_start(out=wch[og % 2], in_=w0th_in[og]).then_inc(s_wch[og % 2], 16)

            def emit_tensor(tensor):
                tensor.wait_ge(S.s_cst, 16 * 7)
                for T in range(NT):
                    waitm(tensor, f"xn{T}", S.s_ap)
                    for g in range(4):
                        for cc in range(8):
                            c = g * 8 + cc
                            ins = tensor.transpose(bankb(4 + g)[:, cc * 128:(cc + 1) * 128],
                                                   xnb[T % 2][:, c * 128:(c + 1) * 128], idb)
                            if cc == 7:
                                ins.then_inc(S.s_ptA, 1)
                tensor.wait_ge(S.s_wqg, 16)
                tensor.wait_ge(S.s_gcst, 16 * 4)
                for th in range(2):
                    if th == 0:
                        waitm(tensor, "xne15", S.s_ap)
                    else:
                        waitm(tensor, f"xne{4 * NT - 1}", S.s_ap)
                    for wi, wten in enumerate([wqs, wk, wv, at_pad]):
                        if th == 0 and wi == 2:
                            waitm(tensor, "xne28", S.s_ap)
                        if th == 0 and wi == 3:
                            waitm(tensor, "xne30", S.s_ap)
                        for c in range(NC):
                            ins = tensor.matmul(
                                PS[:, wi * 1024 + th * 512: wi * 1024 + (th + 1) * 512],
                                wten[:, c * 128:(c + 1) * 128],
                                xnT[:, c * TOK + th * 512:c * TOK + (th + 1) * 512],
                                start=(c == 0), stop=(c == NC - 1))
                            if c == NC - 1 and th == 1:
                                ins.then_inc(S.s_qpe, 1)
                tensor.wait_ge(S.s_qpe, 4)
                waitm(tensor, "qke3", S.s_ap)
                waitm(tensor, "he", S.s_vp)
                for T in range(NT):
                    if T >= 1:
                        waitm(tensor, f"akve{T - 1}", S.s_vp)
                    tensor.transpose(PS[:, 3584:3712], aT_f[:, T * P:(T + 1) * P], idf)
                    tensor.transpose(PS[:, 3712:3840], kT_f[:, T * P:(T + 1) * P],
                                     idf).then_inc(S.s_akvp, 1)
                tensor.wait_ge(S.s_akvp, NT)
                waitm(tensor, f"akve{NT - 1}", S.s_vp)
                waitm(tensor, "v1m", S.s_vp)
                NG = 2 * NT * NQ8
                for g in range(NG + 1):
                    if g < NG:
                        c32 = g // 2
                        if g >= 2:
                            waitm(tensor, f"Eg{g - 2}", S.s_ap)
                        tensor.wait_ge(s_afm[c32 % 2], 32 * (c32 // 2 + 1))
                        for j in range(8):
                            t = 8 * g + j
                            tl = t % 16
                            ins = tensor.matmul(
                                PS[:, 1024 * (g % 2) + 128 * j:1024 * (g % 2) + 128 * (j + 1)],
                                kfm[c32 % 2][0:1, tl * 128:(tl + 1) * 128],
                                afm[c32 % 2][0:1, tl * 128:(tl + 1) * 128],
                                start=True, stop=True)
                            if j == 7:
                                ins.then_inc(S.s_so, 1)
                    if g >= 1:
                        gg = g - 1
                        waitm(tensor, f"Eg{gg}", S.s_ap)
                        for j in range(8):
                            t = 8 * gg + j
                            bnk = 4 + t // 256
                            ins = tensor.matmul(
                                PS[:, bnk * 512 + 2 * (t % 256):bnk * 512 + 2 * (t % 256) + 2],
                                Ebuf[gg % 2][:, 128 * j:128 * (j + 1)],
                                v1[:, 2 * t:2 * t + 2],
                                start=True, stop=True)
                            if j == 7:
                                ins.then_inc(S.s_ec, 1)
                waitm(tensor, "ofm3", S.s_vp)
                tensor.wait_ge(S.s_sel, 16)
                for og in range(NC):
                    tensor.wait_ge(s_wch[og % 2], 16 * (og // 2 + 1))
                    for th in range(2):
                        j = 2 * og + th
                        if og >= 1:
                            waitm(tensor, f"b1{2 * (og - 1) + th}", S.s_ap)
                        for c in range(NC):
                            tensor.matmul(bank(th),
                                          wch[og % 2][:, c * 128:(c + 1) * 128],
                                          xnT[:, c * TOK + th * 512:c * TOK + (th + 1) * 512],
                                          start=(c == 0), stop=False)
                        tensor.matmul(bank(th), selt[:, og * P:(og + 1) * P],
                                      o_fm[:, th * 512:(th + 1) * 512],
                                      start=False, stop=True).then_inc(S.s_peb, 1)
                        if j >= 1:
                            waitm(tensor, f"sel{j - 1}", S.s_vp)
                        for l in range(4):
                            ins = tensor.matmul(
                                bank(2 + l),
                                bt_pad[32 * l:32 * (l + 1), og * P:(og + 1) * P],
                                hT[32 * l:32 * (l + 1), th * 512:(th + 1) * 512],
                                start=True, stop=True, tile_position=(32 * l, 0))
                            if l == 3:
                                ins.then_inc(S.s_pey, 1)
                    tensor.wait_ge(S.s_pey, 2 * og + 2)
                    if og >= 1:
                        waitm(tensor, f"cmb{2 * (og - 1) + 1}", S.s_vp)
                        if og >= 2:
                            waitm(tensor, f"ost{og - 2}", S.s_ap)
                        for T in range(NT):
                            b = 6 + T // 4
                            ins = tensor.transpose(bank(b)[:, (T % 4) * 128:(T % 4 + 1) * 128],
                                                   rest[(og - 1) % 2][:, T * P:(T + 1) * P], idf)
                            if T == NT - 1:
                                ins.then_inc(S.s_ptE, 2)
                waitm(tensor, f"cmb{2 * NC - 1}", S.s_vp)
                waitm(tensor, f"ost{NC - 2}", S.s_ap)
                for T in range(NT):
                    b = 6 + T // 4
                    ins = tensor.transpose(bank(b)[:, (T % 4) * 128:(T % 4 + 1) * 128],
                                           rest[(NC - 1) % 2][:, T * P:(T + 1) * P], idf)
                    if T == NT - 1:
                        ins.then_inc(S.s_ptE, 2)

            return emit_vector, emit_scalar, emit_sync, emit_gpsimd, emit_tensor

        ev, es, _, _, _ = make_pass(counting=True)
        ev(Prox(_Dry(), None, _V_OPS))
        es(Prox(_Dry(), None, _A_OPS))

        ev, es, esync, egp, etens = make_pass(counting=False)
        block = ctx.enter_context(nc.Block())

        @block.sync
        def _(sync):
            esync(sync)

        @block.gpsimd
        def _(gpsimd):
            egp(gpsimd)

        @block.scalar
        def _(scalar):
            es(Prox(scalar, S.s_ap, _A_OPS))

        @block.vector
        def _(vector):
            ev(Prox(vector, S.s_vp, _V_OPS))

        @block.tensor
        def _(tensor):
            etens(tensor)

    return nc


def _host_prep(L0_attn_x, L0_attn_Wq, L0_attn_bq, L0_attn_Wk, L0_attn_bk,
               L0_attn_Wv, L0_attn_bv, L0_ffn_W0, L0_ffn_b0, L0_ffn_A, L0_ffn_B):
    f32 = np.float32
    x = np.ascontiguousarray(np.asarray(L0_attn_x, f32).reshape(8192, D))
    wq_full = np.asarray(L0_attn_Wq, f32).reshape(D, D)
    bq = np.ascontiguousarray(np.asarray(L0_attn_bq, f32).reshape(DQ, NH))
    wk = np.ascontiguousarray(np.asarray(L0_attn_Wk, f32).reshape(D, DQ))
    bk = np.ascontiguousarray(np.asarray(L0_attn_bk, f32).reshape(DQ, 1))
    wv = np.ascontiguousarray(np.asarray(L0_attn_Wv, f32).reshape(D, DQ))
    bv = np.ascontiguousarray(np.asarray(L0_attn_bv, f32).reshape(DQ, 1))
    w0 = np.asarray(L0_ffn_W0, f32)
    b0 = np.asarray(L0_ffn_b0, f32)
    A = np.asarray(L0_ffn_A, f32)
    B = np.asarray(L0_ffn_B, f32)

    w0th = np.ascontiguousarray(
        w0.reshape(NC, P, NC, P).transpose(0, 3, 2, 1).reshape(NC, P, D))
    b0f = np.ascontiguousarray(b0.reshape(NC, P).T)
    at = np.zeros((D, P), dtype=f32)
    for l in range(4):
        at[:, 32 * l:32 * l + 16] = A[l].T
    bt = np.zeros((P, D), dtype=f32)
    for l in range(4):
        bt[32 * l:32 * l + 16, :] = B[l].T

    import jax
    with jax.default_device(jax.devices("cpu")[0]):
        u = np.asarray(jax.random.uniform(
            jax.random.key(42, impl="threefry2x32"), (4, 2048, D)))
    lidx = ((u >= 0.25).astype(np.int8) + (u >= 0.5) + (u >= 0.75)).reshape(8192, D)

    selt = np.zeros((P, D), dtype=ml_dtypes.bfloat16)
    selt[np.arange(D) // NH, np.arange(D)] = 1.0
    idf = np.eye(P, dtype=f32)
    idb = np.eye(P, dtype=ml_dtypes.bfloat16)
    cvec = np.zeros((P, 8), dtype=f32)
    cvec[:, 0] = EPS
    cvec[:, 1] = -0.5
    cvec[:, 2] = SCALE

    in_maps = []
    for c in range(8):
        sl = slice(c * TOK, (c + 1) * TOK)
        xs = x[sl]
        m = dict(x=np.ascontiguousarray(xs), xT=np.ascontiguousarray(xs.T),
                 wq=np.ascontiguousarray(wq_full[512 * c:512 * (c + 1), :]),
                 bq=bq, wk=wk, bk=bk, wv=wv, bv=bv,
                 w0th=w0th, b0f=b0f, at=at, bt=bt,
                 selt=selt, idf=idf, idb=idb, cvec=cvec)
        for l in (1, 2, 3):
            m[f"mt{l}"] = np.ascontiguousarray((lidx[sl] == l).astype(np.uint8).T)
        in_maps.append(m)
    return in_maps


_NC_CACHE = None


def kernel(**inputs):
    global _NC_CACHE
    in_maps = _host_prep(**inputs)
    if _NC_CACHE is None:
        _NC_CACHE = build_nc()
    res = run_bass_kernel_spmd(_NC_CACHE, in_maps, core_ids=list(range(8)))
    out = np.concatenate([r["out"] for r in res.results], axis=0)
    return out.reshape(4, 2048, D)


# version 23
# speedup vs baseline: 1.0509x; 1.0509x over previous
import numpy as np
import ml_dtypes
import concourse.bass as bass
import concourse.mybir as mybir
from concourse.bass_utils import run_bass_kernel_spmd
from contextlib import ExitStack

F32 = mybir.dt.float32
BF16 = mybir.dt.bfloat16
U8 = mybir.dt.uint8
AF = mybir.ActivationFunctionType
ALU = mybir.AluOpType
AX = mybir.AxisListType

P = 128
D = 4096
NT = 8
TOK = NT * P
NC = D // P
NH = 32
DQ = 128
SCALE = 1.0 / float(np.sqrt(128.0))
EPS = 1e-5
NQ8 = 8
DS = DQ // NQ8

_V_OPS = {"bn_stats", "bn_aggr",
          "tensor_tensor", "tensor_add", "tensor_sub", "tensor_mul", "tensor_max",
          "tensor_copy", "tensor_scalar", "tensor_single_scalar", "tensor_scalar_mul",
          "tensor_scalar_add", "scalar_tensor_tensor", "tensor_reduce", "reduce_sum",
          "reduce_max", "reciprocal", "copy_predicated", "select", "memset", "memzero"}
_A_OPS = {"activation", "copy", "add", "mul", "square", "sqrt", "memzero"}


class _DryIns:
    def then_inc(self, *a, **k):
        return self


class _Dry:
    def __getattr__(self, name):
        def f(*a, **kw):
            return _DryIns()
        return f


class Prox:
    def __init__(self, eng, sem, ops):
        self._eng = eng
        self._sem = sem
        self._ops = ops
        self.n = 0

    def __getattr__(self, name):
        fn = getattr(self._eng, name)
        if name in self._ops:
            def wrapped(*a, **kw):
                ins = fn(*a, **kw)
                ins.then_inc(self._sem, 1)
                self.n += 1
                return ins
            return wrapped
        return fn

    def wait_prev(self):
        if self.n > 0:
            self._eng.wait_ge(self._sem, self.n)


def build_nc():
    nc = bass.Bass("TRN2", target_bir_lowering=False)

    x_in = nc.dram_tensor("x", [TOK, D], F32, kind="ExternalInput").ap()
    xT_in = nc.dram_tensor("xT", [D, TOK], F32, kind="ExternalInput").ap()
    wq_in = nc.dram_tensor("wq", [512, D], F32, kind="ExternalInput").ap()
    wqsl_d = nc.dram_tensor("wqsl", [P, 512], BF16, kind="Internal").ap()
    wqsg_d = nc.dram_tensor("wqsg", [8 * P, 512], BF16, kind="Internal").ap()
    bq_in = nc.dram_tensor("bq", [DQ, NH], F32, kind="ExternalInput").ap()
    wk_in = nc.dram_tensor("wk", [D, DQ], F32, kind="ExternalInput").ap()
    bk_in = nc.dram_tensor("bk", [DQ, 1], F32, kind="ExternalInput").ap()
    wv_in = nc.dram_tensor("wv", [D, DQ], F32, kind="ExternalInput").ap()
    bv_in = nc.dram_tensor("bv", [DQ, 1], F32, kind="ExternalInput").ap()
    w0th_in = nc.dram_tensor("w0th", [NC, P, D], F32, kind="ExternalInput").ap()
    b0f_in = nc.dram_tensor("b0f", [P, NC], F32, kind="ExternalInput").ap()
    at_in = nc.dram_tensor("at", [D, P], F32, kind="ExternalInput").ap()
    bt_in = nc.dram_tensor("bt", [P, D], F32, kind="ExternalInput").ap()
    mt_in = [nc.dram_tensor(f"mt{l}", [D, TOK], U8, kind="ExternalInput").ap() for l in (1, 2, 3)]
    selt_in = nc.dram_tensor("selt", [P, D], BF16, kind="ExternalInput").ap()
    idf_in = nc.dram_tensor("idf", [P, P], F32, kind="ExternalInput").ap()
    idb_in = nc.dram_tensor("idb", [P, P], BF16, kind="ExternalInput").ap()
    cvec_in = nc.dram_tensor("cvec", [P, 8], F32, kind="ExternalInput").ap()
    out_d = nc.dram_tensor("out", [TOK, D], F32, kind="ExternalOutput").ap()
    abounce = nc.dram_tensor("abounce", [NT, P, P], BF16, kind="Internal").ap()
    kbounce = nc.dram_tensor("kbounce", [NT, P, P], BF16, kind="Internal").ap()

    with ExitStack() as ctx:
        def sb(name, shape, dt):
            return ctx.enter_context(nc.sbuf_tensor(name, shape, dt))[:]

        idf = sb("idf_s", [P, P], F32)
        idb = sb("idb_s", [P, P], BF16)
        selt = sb("selt_s", [P, D], BF16)
        wqs = sb("wqs", [P, D], BF16)
        wk = sb("wk_s", [P, D], BF16)
        wv = sb("wv_s", [P, D], BF16)
        at_pad = sb("at_pad", [P, D], BF16)
        bt_pad = sb("bt_pad", [P, D], BF16)
        b0f = sb("b0f_s", [P, NC], F32)
        bq_sb = sb("bq_sb", [DQ, NH], F32)
        bkv = sb("bkv", [DQ, 2], F32)
        stats = sb("stats", [P, 64], F32)
        bqs = sb("bqs_c", [P, 2], F32)
        cvec = sb("cvec_s", [P, 8], F32)
        xnT = sb("xnT", [P, NC * TOK], BF16)
        hT = sb("hT", [P, TOK], BF16)
        o_fm = sb("o_fm", [P, TOK], BF16)
        a_tm = sb("a_tm", [P, TOK], BF16)
        k_tm = sb("k_tm", [P, TOK], BF16)
        v1 = sb("v1", [P, 2 * TOK], BF16)
        rzb = sb("rzb", [P, 256], F32)

        AR1 = sb("AR1", [P, 32768], U8)
        AR2 = sb("AR2", [P, 32768], U8)
        AR3 = sb("AR3", [P, 16384], U8)

        def view(ar, off, shape, dt):
            v = ar[:shape[0], off:off + int(np.prod(shape[1:])) * mybir.dt.size(dt)].bitcast(dt)
            if len(shape) == 3:
                v = v.rearrange("p (a b) -> p a b", a=shape[1])
            return v

        xload = [view(AR1, 0, [P, D], F32), view(AR1, 16384, [P, D], F32)]
        junkb = view(AR3, 0, [P, D], BF16)
        xn_buf = view(AR3, 8192, [P, D], BF16)
        wqc = [view(AR2, b * 8192, [P, 2048], F32) for b in range(4)]
        aT_f = view(AR1, 0, [P, TOK], F32)
        kT_f = view(AR1, 4096, [P, TOK], F32)
        vT_f = view(AR1, 8192, [P, TOK], F32)
        Ebuf = [view(AR3, 0, [P, TOK], BF16), view(AR3, 2048, [P, TOK], BF16)]
        numZ = view(AR3, 4096, [P, 2048], F32)
        afm = [AR1[0:1, 12288 + b * 4096:12288 + (b + 1) * 4096].bitcast(BF16) for b in range(2)]
        kfm = [AR1[0:1, 20480 + b * 4096:20480 + (b + 1) * 4096].bitcast(BF16) for b in range(2)]
        xTb = [view(AR1, 0, [P, TOK], F32), view(AR1, 4096, [P, TOK], F32)]
        rest = [view(AR1, 8192, [P, TOK], F32), view(AR1, 12288, [P, TOK], F32)]
        outst = [view(AR1, 16384, [P, TOK], F32), view(AR1, 20480, [P, TOK], F32)]
        base1 = [view(AR1, 24576, [P, 512], F32), view(AR1, 26624, [P, 512], F32)]
        selb = view(AR1, 28672, [P, 512], F32)
        wch = [view(AR2, 0, [P, D], BF16), view(AR2, 8192, [P, D], BF16)]
        mtb = [[view(AR2, 24576 + (pl * 2 + b) * 1024, [P, TOK], U8) for b in range(2)]
               for pl in range(3)]
        bnst = sb("bnst", [P, 48], F32)
        xnb = [xn_buf, junkb]

        PS = ctx.enter_context(nc.psum_tensor("PS", [P, 4096], F32))[:]
        PSb = PS.bitcast(BF16)
        bank = lambda b: PS[:, 512 * b:512 * (b + 1)]
        bankb = lambda b: PSb[:, 1024 * b:1024 * (b + 1)]

        sems = {}
        for name in ["s_vp", "s_ap",
                     "s_xa0", "s_xa1", "s_wqc0", "s_wqc1", "s_wqc2", "s_wqc3",
                     "s_cst", "s_gcst", "s_tr", "s_sel", "s_cv",
                     "s_ptA", "s_qpe", "s_akvp",
                     "s_ad", "s_so", "s_ec", "s_afm0", "s_afm1",
                     "s_wql", "s_ccv", "s_wqg",
                     "s_wch0", "s_wch1", "s_mt0", "s_mt1", "s_xt0", "s_xt1",
                     "s_peb", "s_pey", "s_ptE", "s_out"]:
            sems[name] = ctx.enter_context(nc.semaphore(name))
        S = type("SS", (), sems)
        s_wqc = [S.s_wqc0, S.s_wqc1, S.s_wqc2, S.s_wqc3]
        s_xa = [S.s_xa0, S.s_xa1]
        s_afm = [S.s_afm0, S.s_afm1]
        s_wch = [S.s_wch0, S.s_wch1]
        s_mt = [S.s_mt0, S.s_mt1]
        s_xt = [S.s_xt0, S.s_xt1]

        marks = {}
        cT = lambda T, i: stats[:, 8 * T + i:8 * T + i + 1]

        def make_pass(counting):
            def mark(name, px):
                if counting:
                    marks[name] = px.n
                else:
                    assert marks[name] == px.n, (name, marks[name], px.n)

            def waitm(eng, name, sem):
                if not counting:
                    eng.wait_ge(sem, marks[name])

            def wq_tree(vector, k2):
                vector.wait_ge(s_wqc[k2 % 4], 16 * (k2 // 4 + 1))
                v32 = wqc[k2 % 4].rearrange("p (d h) -> p d h", d=64)
                k, h = k2 // 2, k2 % 2
                with nc.allow_low_precision(reason="head-sum output feeds bf16 matmul"):
                    vector.reduce_sum(wqs[:, k * 128 + h * 64:k * 128 + (h + 1) * 64],
                                      v32, axis=AX.X)
                mark(f"tree{k2}", vector)

            def xnT_evac(scalar, T):
                for g in range(4):
                    i = 4 * T + g
                    scalar.wait_ge(S.s_ptA, i + 1)
                    dst = xnT.rearrange("p (c t) -> p c t", c=NC)[:, g * 8:(g + 1) * 8,
                                                                 T * P:(T + 1) * P]
                    src = bankb(4 + g).rearrange("p (c t) -> p c t", c=8)
                    scalar.activation(dst, src, AF.Copy)
                    mark(f"xne{i}", scalar)

            def emit_vector(vector):
                for T in range(NT):
                    vector.wait_ge(s_xa[T % 2], 32 * (T // 2 + 1))
                    for g in range(8):
                        vector.bn_stats(bnst[:, 6 * g:6 * (g + 1)],
                                        xload[T % 2][:, 512 * g:512 * (g + 1)])
                    vector.wait_prev()
                    vector.bn_aggr(stats[:, 8 * T + 2:8 * T + 4], bnst)
                    mark(f"lnd{T}", vector)
                    if T < 4:
                        wq_tree(vector, 2 * T)
                        wq_tree(vector, 2 * T + 1)
                    waitm(vector, f"rst{T}", S.s_ap)
                    vector.wait_prev()
                    vector.scalar_tensor_tensor(cT(T, 6), cT(T, 2), -1.0, cT(T, 5),
                                                ALU.mult, ALU.mult)
                    mark(f"bia{T}", vector)
                vector.wait_ge(S.s_cst, 16 * 6)
                vector.reduce_sum(bqs[:, 1:2], bq_sb, axis=AX.X)
                vector.wait_prev()
                vector.tensor_scalar_mul(bqs[:, 0:1], bqs[:, 1:2], float(SCALE))
                mark("bqs", vector)
                vector.wait_ge(S.s_qpe, 3)
                vector.tensor_scalar_mul(hT, PS[:, 3072:4096], 8.0)
                mark("he", vector)
                for T in range(NT):
                    vector.wait_ge(S.s_akvp, T + 1)
                    vector.tensor_copy(a_tm[:, T * P:(T + 1) * P], PS[:, 3584:3712])
                    vector.tensor_copy(k_tm[:, T * P:(T + 1) * P], PS[:, 3712:3840])
                    mark(f"akve{T}", vector)
                vector.memset(v1, 1.0)
                vector.wait_prev()
                vector.tensor_copy(
                    v1.rearrange("p (t two) -> p t two", two=2)[:, :, 0:1],
                    vT_f[:, :, None])
                mark("v1m", vector)
                for b in range(4):
                    vector.wait_ge(S.s_ec, 32 * (b + 1))
                    vector.tensor_copy(numZ[:, b * 512:(b + 1) * 512], bank(4 + b))
                    nzv = numZ[:, b * 512:(b + 1) * 512].rearrange(
                        "p (t two) -> p t two", two=2)
                    vector.wait_prev()
                    vector.reciprocal(rzb, nzv[:, :, 1:2].rearrange("p t two -> p (t two)"))
                    vector.wait_prev()
                    vector.tensor_tensor(o_fm[:, b * 256:(b + 1) * 256],
                                         nzv[:, :, 0:1].rearrange("p t two -> p (t two)"),
                                         rzb, ALU.mult)
                    mark(f"ofm{b}", vector)
                for og in range(NC):
                    for th in range(2):
                        j = 2 * og + th
                        vector.wait_ge(S.s_pey, j + 1)
                        vector.wait_ge(s_mt[og % 2], 48 * (og // 2 + 1))
                        vector.tensor_copy(selb, bank(2))
                        for pl in range(3):
                            vector.wait_prev()
                            vector.copy_predicated(
                                selb, mtb[pl][og % 2][:, th * 512:(th + 1) * 512],
                                bank(3 + pl))
                        mark(f"sel{j}", vector)
                        waitm(vector, f"b1{j}", S.s_ap)
                        vector.wait_ge(s_xt[og % 2], 16 * (og // 2 + 1))
                        if og >= 2 and th == 0:
                            vector.wait_ge(S.s_ptE, 2 * (og - 1))
                        rst = rest[og % 2][:, th * 512:(th + 1) * 512]
                        vector.wait_prev()
                        vector.scalar_tensor_tensor(rst, base1[j % 2], 1.0, selb,
                                                    ALU.mult, ALU.add)
                        vector.wait_prev()
                        vector.scalar_tensor_tensor(
                            rst, xTb[og % 2][:, th * 512:(th + 1) * 512], 1.0, rst,
                            ALU.mult, ALU.add)
                        mark(f"cmb{j}", vector)

            def emit_scalar(scalar):
                for T in range(NT):
                    if T >= 1:
                        xnT_evac(scalar, T - 1)
                    if T == 0:
                        scalar.wait_ge(S.s_cv, 16)
                    waitm(scalar, f"lnd{T}", S.s_vp)
                    scalar.activation(cT(T, 4), cT(T, 3), AF.Ln, bias=cvec[:, 0:1])
                    scalar.wait_prev()
                    scalar.activation(cT(T, 5), cT(T, 4), AF.Exp, scale=cvec[:, 1:2])
                    mark(f"rst{T}", scalar)
                    waitm(scalar, f"bia{T}", S.s_vp)
                    if T >= 2:
                        scalar.wait_ge(S.s_ptA, 4 * (T - 1))
                    scalar.wait_prev()
                    scalar.activation(xnb[T % 2], xload[T % 2], AF.Identity,
                                      bias=cT(T, 6), scale=cT(T, 5))
                    mark(f"xn{T}", scalar)
                xnT_evac(scalar, NT - 1)
                scalar.wait_ge(S.s_qpe, 4)
                waitm(scalar, "bqs", S.s_vp)
                scalar.activation(aT_f, PS[:, 0:1024], AF.Identity,
                                  bias=bqs[:, 0:1], scale=cvec[:, 2:3])
                mark("qke1", scalar)
                scalar.wait_ge(S.s_qpe, 1)
                scalar.activation(kT_f, PS[:, 1024:2048], AF.Identity, bias=bkv[:, 0:1])
                mark("qke2", scalar)
                scalar.wait_ge(S.s_qpe, 2)
                scalar.activation(vT_f, PS[:, 2048:3072], AF.Identity, bias=bkv[:, 1:2])
                mark("qke3", scalar)
                for g in range(2 * NT * NQ8):
                    scalar.wait_ge(S.s_so, g + 1)
                    if g >= 2:
                        scalar.wait_ge(S.s_ec, g - 1)
                    scalar.activation(Ebuf[g % 2], PS[:, 1024 * (g % 2):1024 * (g % 2) + 1024],
                                      AF.Exp)
                    mark(f"Eg{g}", scalar)
                for og in range(NC):
                    for th in range(2):
                        j = 2 * og + th
                        scalar.wait_ge(S.s_peb, j + 1)
                        if j >= 2:
                            waitm(scalar, f"cmb{j - 2}", S.s_vp)
                        scalar.activation(base1[j % 2], bank(th),
                                          AF.Identity, bias=b0f[:, og:og + 1])
                        mark(f"b1{j}", scalar)
                    if og >= 1:
                        scalar.wait_ge(S.s_ptE, 2 * og)
                        if og >= 2:
                            scalar.wait_ge(S.s_out, 16 * (og - 1))
                        scalar.activation(outst[(og - 1) % 2][:, 0:512], bank(6), AF.Copy)
                        scalar.wait_prev()
                        scalar.activation(outst[(og - 1) % 2][:, 512:1024], bank(7), AF.Copy)
                        mark(f"ost{og - 1}", scalar)
                scalar.wait_ge(S.s_ptE, 2 * NC)
                scalar.wait_ge(S.s_out, 16 * (NC - 1))
                scalar.activation(outst[(NC - 1) % 2][:, 0:512], bank(6), AF.Copy)
                scalar.wait_prev()
                scalar.activation(outst[(NC - 1) % 2][:, 512:1024], bank(7), AF.Copy)
                mark(f"ost{NC - 1}", scalar)

            def emit_sync(sync):
                sync.dma_start(out=cvec, in_=cvec_in).then_inc(S.s_cv, 16)
                for hh in range(2):
                    sync.dma_start(out=xload[0][hh * 64:(hh + 1) * 64, :],
                                   in_=x_in[hh * 64:hh * 64 + 64, :]).then_inc(s_xa[0], 16)
                sync.dma_start(out=idf, in_=idf_in).then_inc(S.s_cst, 16)
                sync.dma_start(out=idb, in_=idb_in).then_inc(S.s_cst, 16)
                sync.wait_ge(s_xa[0], 32)
                for hh in range(2):
                    sync.dma_start(out=xload[1][hh * 64:(hh + 1) * 64, :],
                                   in_=x_in[P + hh * 64:P + hh * 64 + 64, :]).then_inc(s_xa[1], 16)
                sync.dma_start(out=bq_sb, in_=bq_in).then_inc(S.s_cst, 16)
                sync.dma_start(out=bkv[:, 0:1], in_=bk_in).then_inc(S.s_cst, 16)
                sync.dma_start(out=bkv[:, 1:2], in_=bv_in).then_inc(S.s_cst, 16)
                sync.dma_start(out=b0f, in_=b0f_in).then_inc(S.s_cst, 16)
                def wq_half(k2):
                    k, h = k2 // 2, k2 % 2
                    if k2 >= 4:
                        waitm(sync, f"tree{k2 - 4}", S.s_vp)
                    sync.dma_start(
                        out=wqc[k2 % 4],
                        in_=wq_in[k * P:(k + 1) * P, h * 2048:(h + 1) * 2048],
                    ).then_inc(s_wqc[k2 % 4], 16)
                for k2 in range(2):
                    wq_half(k2)
                sync.dma_start(out=selt, in_=selt_in).then_inc(S.s_sel, 16)
                for k2 in range(2, 4):
                    wq_half(k2)
                for k in range(2, NT):
                    waitm(sync, f"xn{k - 2}", S.s_ap)
                    for hh in range(2):
                        sync.dma_start(out=xload[k % 2][hh * 64:(hh + 1) * 64, :],
                                       in_=x_in[k * P + hh * 64:k * P + hh * 64 + 64, :],
                                       ).then_inc(s_xa[k % 2], 16)
                    if k in (2, 3):
                        wq_half(2 * k)
                        wq_half(2 * k + 1)
                sync.wait_ge(S.s_ccv, 1)
                sync.dma_start(
                    out=wqs.rearrange("p (r j d) -> p r j d", r=8, j=4),
                    in_=wqsg_d.rearrange("(r p) (j d) -> p r j d", p=P, j=4),
                ).then_inc(S.s_wqg, 16)
                for c32 in range(64):
                    T = c32 // 8
                    sync.wait_ge(S.s_ad, 32 * (T + 1))
                    if c32 >= 2:
                        sync.wait_ge(S.s_so, 2 * (c32 - 1))
                    off = (c32 % 8) * 2048
                    sync.dma_start(out=afm[c32 % 2],
                                   in_=abounce[T].rearrange("t d -> (t d)")[off:off + 2048][None, :],
                                   ).then_inc(s_afm[c32 % 2], 16)
                    sync.dma_start(out=kfm[c32 % 2],
                                   in_=kbounce[T].rearrange("t d -> (t d)")[off:off + 2048][None, :],
                                   ).then_inc(s_afm[c32 % 2], 16)
                for og in range(NC):
                    if og == 0:
                        waitm(sync, "tree7", S.s_vp)
                    if og >= 2:
                        waitm(sync, f"sel{2 * (og - 2) + 1}", S.s_vp)
                    for pl in range(3):
                        sync.dma_start(out=mtb[pl][og % 2],
                                       in_=mt_in[pl][og * P:(og + 1) * P, :]).then_inc(s_mt[og % 2], 16)
                    if og == 0:
                        sync.wait_ge(S.s_akvp, NT)
                        waitm(sync, "v1m", S.s_vp)
                    if og >= 2:
                        waitm(sync, f"cmb{2 * (og - 2) + 1}", S.s_vp)
                    sync.dma_start(out=xTb[og % 2],
                                   in_=xT_in[og * P:(og + 1) * P, :]).then_inc(s_xt[og % 2], 16)
                    if og >= 1:
                        waitm(sync, f"ost{og - 1}", S.s_ap)
                        sync.dma_start(
                            out=out_d[:, (og - 1) * P:og * P].rearrange("(T t) j -> t T j", t=P),
                            in_=outst[(og - 1) % 2].rearrange("p (T j) -> p T j", T=NT),
                        ).then_inc(S.s_out, 16)
                waitm(sync, f"ost{NC - 1}", S.s_ap)
                sync.dma_start(
                    out=out_d[:, (NC - 1) * P:NC * P].rearrange("(T t) j -> t T j", t=P),
                    in_=outst[(NC - 1) % 2].rearrange("p (T j) -> p T j", T=NT),
                ).then_inc(S.s_out, 16)
                sync.wait_ge(S.s_out, 16 * NC)

            def emit_gpsimd(gpsimd):
                waitm(gpsimd, "tree7", S.s_vp)
                gpsimd.dma_start(out=wqsl_d, in_=wqs[:, 0:512]).then_inc(S.s_wql, 16)
                gpsimd.wait_ge(S.s_wql, 16)
                gpsimd.collective_compute(
                    "AllGather", ALU.bypass,
                    replica_groups=[list(range(8))],
                    ins=[wqsl_d], outs=[wqsg_d]).then_inc(S.s_ccv)
                gpsimd.dma_start(out=wk, in_=wk_in.rearrange("(c p) d -> p c d", p=P),
                                 ).then_inc(S.s_gcst, 16)
                gpsimd.dma_start(out=wv, in_=wv_in.rearrange("(c p) d -> p c d", p=P),
                                 ).then_inc(S.s_gcst, 16)
                gpsimd.dma_start(out=at_pad, in_=at_in.rearrange("(c p) q -> p c q", p=P),
                                 ).then_inc(S.s_gcst, 16)
                gpsimd.dma_start(out=bt_pad, in_=bt_in).then_inc(S.s_gcst, 16)
                for T in range(NT):
                    waitm(gpsimd, f"akve{T}", S.s_vp)
                    if T >= 1:
                        gpsimd.wait_ge(S.s_ad, 32 * T)
                    gpsimd.dma_start(out=abounce[T],
                                     in_=a_tm[:, T * P:(T + 1) * P]).then_inc(S.s_ad, 16)
                    gpsimd.dma_start(out=kbounce[T],
                                     in_=k_tm[:, T * P:(T + 1) * P]).then_inc(S.s_ad, 16)
                for og in range(NC):
                    if og == 0:
                        waitm(gpsimd, "tree7", S.s_vp)
                    if og >= 2:
                        gpsimd.wait_ge(S.s_peb, 2 * (og - 1))
                    gpsimd.dma_start(out=wch[og % 2], in_=w0th_in[og]).then_inc(s_wch[og % 2], 16)

            def emit_tensor(tensor):
                tensor.wait_ge(S.s_cst, 16 * 6)
                for T in range(NT):
                    waitm(tensor, f"xn{T}", S.s_ap)
                    for g in range(4):
                        for cc in range(8):
                            c = g * 8 + cc
                            ins = tensor.transpose(bankb(4 + g)[:, cc * 128:(cc + 1) * 128],
                                                   xnb[T % 2][:, c * 128:(c + 1) * 128], idb)
                            if cc == 7:
                                ins.then_inc(S.s_ptA, 1)
                tensor.wait_ge(S.s_gcst, 16 * 4)
                for th in range(2):
                    if th == 0:
                        waitm(tensor, "xne15", S.s_ap)
                    else:
                        waitm(tensor, f"xne{4 * NT - 1}", S.s_ap)
                    for wi, wten in [(1, wk), (2, wv), (3, at_pad), (0, wqs)]:
                        if th == 0 and wi == 2:
                            waitm(tensor, "xne28", S.s_ap)
                        if th == 0 and wi == 3:
                            waitm(tensor, "xne30", S.s_ap)
                        if th == 0 and wi == 0:
                            tensor.wait_ge(S.s_wqg, 16)
                        for c in range(NC):
                            ins = tensor.matmul(
                                PS[:, wi * 1024 + th * 512: wi * 1024 + (th + 1) * 512],
                                wten[:, c * 128:(c + 1) * 128],
                                xnT[:, c * TOK + th * 512:c * TOK + (th + 1) * 512],
                                start=(c == 0), stop=(c == NC - 1))
                            if c == NC - 1 and th == 1:
                                ins.then_inc(S.s_qpe, 1)
                tensor.wait_ge(S.s_qpe, 4)
                waitm(tensor, "qke3", S.s_ap)
                waitm(tensor, "he", S.s_vp)
                for T in range(NT):
                    if T >= 1:
                        waitm(tensor, f"akve{T - 1}", S.s_vp)
                    tensor.transpose(PS[:, 3584:3712], aT_f[:, T * P:(T + 1) * P], idf)
                    tensor.transpose(PS[:, 3712:3840], kT_f[:, T * P:(T + 1) * P],
                                     idf).then_inc(S.s_akvp, 1)
                tensor.wait_ge(S.s_akvp, NT)
                waitm(tensor, f"akve{NT - 1}", S.s_vp)
                waitm(tensor, "v1m", S.s_vp)
                NG = 2 * NT * NQ8
                for g in range(NG + 1):
                    if g < NG:
                        c32 = g // 2
                        if g >= 2:
                            waitm(tensor, f"Eg{g - 2}", S.s_ap)
                        tensor.wait_ge(s_afm[c32 % 2], 32 * (c32 // 2 + 1))
                        for j in range(8):
                            t = 8 * g + j
                            tl = t % 16
                            ins = tensor.matmul(
                                PS[:, 1024 * (g % 2) + 128 * j:1024 * (g % 2) + 128 * (j + 1)],
                                kfm[c32 % 2][0:1, tl * 128:(tl + 1) * 128],
                                afm[c32 % 2][0:1, tl * 128:(tl + 1) * 128],
                                start=True, stop=True)
                            if j == 7:
                                ins.then_inc(S.s_so, 1)
                    if g >= 1:
                        gg = g - 1
                        waitm(tensor, f"Eg{gg}", S.s_ap)
                        for j in range(8):
                            t = 8 * gg + j
                            bnk = 4 + t // 256
                            ins = tensor.matmul(
                                PS[:, bnk * 512 + 2 * (t % 256):bnk * 512 + 2 * (t % 256) + 2],
                                Ebuf[gg % 2][:, 128 * j:128 * (j + 1)],
                                v1[:, 2 * t:2 * t + 2],
                                start=True, stop=True)
                            if j == 7:
                                ins.then_inc(S.s_ec, 1)
                waitm(tensor, "ofm3", S.s_vp)
                tensor.wait_ge(S.s_sel, 16)
                for og in range(NC):
                    tensor.wait_ge(s_wch[og % 2], 16 * (og // 2 + 1))
                    for th in range(2):
                        j = 2 * og + th
                        if og >= 1:
                            waitm(tensor, f"b1{2 * (og - 1) + th}", S.s_ap)
                        for c in range(NC):
                            tensor.matmul(bank(th),
                                          wch[og % 2][:, c * 128:(c + 1) * 128],
                                          xnT[:, c * TOK + th * 512:c * TOK + (th + 1) * 512],
                                          start=(c == 0), stop=False)
                        tensor.matmul(bank(th), selt[:, og * P:(og + 1) * P],
                                      o_fm[:, th * 512:(th + 1) * 512],
                                      start=False, stop=True).then_inc(S.s_peb, 1)
                        if j >= 1:
                            waitm(tensor, f"sel{j - 1}", S.s_vp)
                        for l in range(4):
                            ins = tensor.matmul(
                                bank(2 + l),
                                bt_pad[32 * l:32 * (l + 1), og * P:(og + 1) * P],
                                hT[32 * l:32 * (l + 1), th * 512:(th + 1) * 512],
                                start=True, stop=True, tile_position=(32 * l, 0))
                            if l == 3:
                                ins.then_inc(S.s_pey, 1)
                    tensor.wait_ge(S.s_pey, 2 * og + 2)
                    if og >= 1:
                        waitm(tensor, f"cmb{2 * (og - 1) + 1}", S.s_vp)
                        if og >= 2:
                            waitm(tensor, f"ost{og - 2}", S.s_ap)
                        for T in range(NT):
                            b = 6 + T // 4
                            ins = tensor.transpose(bank(b)[:, (T % 4) * 128:(T % 4 + 1) * 128],
                                                   rest[(og - 1) % 2][:, T * P:(T + 1) * P], idf)
                            if T == NT - 1:
                                ins.then_inc(S.s_ptE, 2)
                waitm(tensor, f"cmb{2 * NC - 1}", S.s_vp)
                waitm(tensor, f"ost{NC - 2}", S.s_ap)
                for T in range(NT):
                    b = 6 + T // 4
                    ins = tensor.transpose(bank(b)[:, (T % 4) * 128:(T % 4 + 1) * 128],
                                           rest[(NC - 1) % 2][:, T * P:(T + 1) * P], idf)
                    if T == NT - 1:
                        ins.then_inc(S.s_ptE, 2)

            return emit_vector, emit_scalar, emit_sync, emit_gpsimd, emit_tensor

        ev, es, _, _, _ = make_pass(counting=True)
        ev(Prox(_Dry(), None, _V_OPS))
        es(Prox(_Dry(), None, _A_OPS))

        ev, es, esync, egp, etens = make_pass(counting=False)
        block = ctx.enter_context(nc.Block())

        @block.sync
        def _(sync):
            esync(sync)

        @block.gpsimd
        def _(gpsimd):
            egp(gpsimd)

        @block.scalar
        def _(scalar):
            es(Prox(scalar, S.s_ap, _A_OPS))

        @block.vector
        def _(vector):
            ev(Prox(vector, S.s_vp, _V_OPS))

        @block.tensor
        def _(tensor):
            etens(tensor)

    return nc


def _host_prep(L0_attn_x, L0_attn_Wq, L0_attn_bq, L0_attn_Wk, L0_attn_bk,
               L0_attn_Wv, L0_attn_bv, L0_ffn_W0, L0_ffn_b0, L0_ffn_A, L0_ffn_B):
    f32 = np.float32
    x = np.ascontiguousarray(np.asarray(L0_attn_x, f32).reshape(8192, D))
    wq_full = np.asarray(L0_attn_Wq, f32).reshape(D, D)
    bq = np.ascontiguousarray(np.asarray(L0_attn_bq, f32).reshape(DQ, NH))
    wk = np.ascontiguousarray(np.asarray(L0_attn_Wk, f32).reshape(D, DQ))
    bk = np.ascontiguousarray(np.asarray(L0_attn_bk, f32).reshape(DQ, 1))
    wv = np.ascontiguousarray(np.asarray(L0_attn_Wv, f32).reshape(D, DQ))
    bv = np.ascontiguousarray(np.asarray(L0_attn_bv, f32).reshape(DQ, 1))
    w0 = np.asarray(L0_ffn_W0, f32)
    b0 = np.asarray(L0_ffn_b0, f32)
    A = np.asarray(L0_ffn_A, f32)
    B = np.asarray(L0_ffn_B, f32)

    w0th = np.ascontiguousarray(
        w0.reshape(NC, P, NC, P).transpose(0, 3, 2, 1).reshape(NC, P, D))
    b0f = np.ascontiguousarray(b0.reshape(NC, P).T)
    at = np.zeros((D, P), dtype=f32)
    for l in range(4):
        at[:, 32 * l:32 * l + 16] = A[l].T
    bt = np.zeros((P, D), dtype=f32)
    for l in range(4):
        bt[32 * l:32 * l + 16, :] = B[l].T

    import jax
    with jax.default_device(jax.devices("cpu")[0]):
        u = np.asarray(jax.random.uniform(
            jax.random.key(42, impl="threefry2x32"), (4, 2048, D)))
    lidx = ((u >= 0.25).astype(np.int8) + (u >= 0.5) + (u >= 0.75)).reshape(8192, D)

    selt = np.zeros((P, D), dtype=ml_dtypes.bfloat16)
    selt[np.arange(D) // NH, np.arange(D)] = 1.0
    idf = np.eye(P, dtype=f32)
    idb = np.eye(P, dtype=ml_dtypes.bfloat16)
    cvec = np.zeros((P, 8), dtype=f32)
    cvec[:, 0] = EPS
    cvec[:, 1] = -0.5
    cvec[:, 2] = SCALE

    in_maps = []
    for c in range(8):
        sl = slice(c * TOK, (c + 1) * TOK)
        xs = x[sl]
        m = dict(x=np.ascontiguousarray(xs), xT=np.ascontiguousarray(xs.T),
                 wq=np.ascontiguousarray(wq_full[512 * c:512 * (c + 1), :]),
                 bq=bq, wk=wk, bk=bk, wv=wv, bv=bv,
                 w0th=w0th, b0f=b0f, at=at, bt=bt,
                 selt=selt, idf=idf, idb=idb, cvec=cvec)
        for l in (1, 2, 3):
            m[f"mt{l}"] = np.ascontiguousarray((lidx[sl] == l).astype(np.uint8).T)
        in_maps.append(m)
    return in_maps


_NC_CACHE = None


def kernel(**inputs):
    global _NC_CACHE
    in_maps = _host_prep(**inputs)
    if _NC_CACHE is None:
        _NC_CACHE = build_nc()
    res = run_bass_kernel_spmd(_NC_CACHE, in_maps, core_ids=list(range(8)))
    out = np.concatenate([r["out"] for r in res.results], axis=0)
    return out.reshape(4, 2048, D)


# version 24
# speedup vs baseline: 1.0718x; 1.0199x over previous
import numpy as np
import ml_dtypes
import concourse.bass as bass
import concourse.mybir as mybir
from concourse.bass_utils import run_bass_kernel_spmd
from contextlib import ExitStack

F32 = mybir.dt.float32
BF16 = mybir.dt.bfloat16
U8 = mybir.dt.uint8
AF = mybir.ActivationFunctionType
ALU = mybir.AluOpType
AX = mybir.AxisListType

P = 128
D = 4096
NT = 8
TOK = NT * P
NC = D // P
NH = 32
DQ = 128
SCALE = 1.0 / float(np.sqrt(128.0))
EPS = 1e-5
NQ8 = 8
DS = DQ // NQ8

_V_OPS = {"bn_stats", "bn_aggr",
          "tensor_tensor", "tensor_add", "tensor_sub", "tensor_mul", "tensor_max",
          "tensor_copy", "tensor_scalar", "tensor_single_scalar", "tensor_scalar_mul",
          "tensor_scalar_add", "scalar_tensor_tensor", "tensor_reduce", "reduce_sum",
          "reduce_max", "reciprocal", "copy_predicated", "select", "memset", "memzero"}
_A_OPS = {"activation", "copy", "add", "mul", "square", "sqrt", "memzero"}


class _DryIns:
    def then_inc(self, *a, **k):
        return self


class _Dry:
    def __getattr__(self, name):
        def f(*a, **kw):
            return _DryIns()
        return f


class Prox:
    def __init__(self, eng, sem, ops):
        self._eng = eng
        self._sem = sem
        self._ops = ops
        self.n = 0

    def __getattr__(self, name):
        fn = getattr(self._eng, name)
        if name in self._ops:
            def wrapped(*a, **kw):
                ins = fn(*a, **kw)
                ins.then_inc(self._sem, 1)
                self.n += 1
                return ins
            return wrapped
        return fn

    def wait_prev(self):
        if self.n > 0:
            self._eng.wait_ge(self._sem, self.n)


def build_nc():
    nc = bass.Bass("TRN2", target_bir_lowering=False)

    x_in = nc.dram_tensor("x", [TOK, D], F32, kind="ExternalInput").ap()
    xT_in = nc.dram_tensor("xT", [D, TOK], F32, kind="ExternalInput").ap()
    wq_in = nc.dram_tensor("wq", [512, D], F32, kind="ExternalInput").ap()
    wqsl_d = nc.dram_tensor("wqsl", [P, 512], BF16, kind="Internal").ap()
    wqsg_d = nc.dram_tensor("wqsg", [8 * P, 512], BF16, kind="Internal").ap()
    bq_in = nc.dram_tensor("bq", [DQ, NH], F32, kind="ExternalInput").ap()
    wk_in = nc.dram_tensor("wk", [D, DQ], F32, kind="ExternalInput").ap()
    bk_in = nc.dram_tensor("bk", [DQ, 1], F32, kind="ExternalInput").ap()
    wv_in = nc.dram_tensor("wv", [D, DQ], F32, kind="ExternalInput").ap()
    bv_in = nc.dram_tensor("bv", [DQ, 1], F32, kind="ExternalInput").ap()
    w0th_in = nc.dram_tensor("w0th", [NC, P, D], F32, kind="ExternalInput").ap()
    b0f_in = nc.dram_tensor("b0f", [P, NC], F32, kind="ExternalInput").ap()
    at_in = nc.dram_tensor("at", [D, P], F32, kind="ExternalInput").ap()
    bt_in = nc.dram_tensor("bt", [P, D], F32, kind="ExternalInput").ap()
    mt_in = [nc.dram_tensor(f"mt{l}", [D, TOK], U8, kind="ExternalInput").ap() for l in (1, 2, 3)]
    selt_in = nc.dram_tensor("selt", [P, D], BF16, kind="ExternalInput").ap()
    idf_in = nc.dram_tensor("idf", [P, P], F32, kind="ExternalInput").ap()
    idb_in = nc.dram_tensor("idb", [P, P], BF16, kind="ExternalInput").ap()
    cvec_in = nc.dram_tensor("cvec", [P, 8], F32, kind="ExternalInput").ap()
    out_d = nc.dram_tensor("out", [TOK, D], F32, kind="ExternalOutput").ap()
    abounce = nc.dram_tensor("abounce", [NT, P, P], BF16, kind="Internal").ap()
    kbounce = nc.dram_tensor("kbounce", [NT, P, P], BF16, kind="Internal").ap()

    with ExitStack() as ctx:
        def sb(name, shape, dt):
            return ctx.enter_context(nc.sbuf_tensor(name, shape, dt))[:]

        idf = sb("idf_s", [P, P], F32)
        idb = sb("idb_s", [P, P], BF16)
        selt = sb("selt_s", [P, D], BF16)
        wqs = sb("wqs", [P, D], BF16)
        wk = sb("wk_s", [P, D], BF16)
        wv = sb("wv_s", [P, D], BF16)
        at_pad = sb("at_pad", [P, D], BF16)
        bt_pad = sb("bt_pad", [P, D], BF16)
        b0f = sb("b0f_s", [P, NC], F32)
        bq_sb = sb("bq_sb", [DQ, NH], F32)
        bkv = sb("bkv", [DQ, 2], F32)
        stats = sb("stats", [P, 64], F32)
        bqs = sb("bqs_c", [P, 2], F32)
        cvec = sb("cvec_s", [P, 8], F32)
        xnT = sb("xnT", [P, NC * TOK], BF16)
        hT = sb("hT", [P, TOK], BF16)
        o_fm = sb("o_fm", [P, TOK], BF16)
        a_tm = sb("a_tm", [P, TOK], BF16)
        k_tm = sb("k_tm", [P, TOK], BF16)
        v1 = sb("v1", [P, 2 * TOK], BF16)
        rzb = sb("rzb", [P, 256], F32)

        AR1 = sb("AR1", [P, 32768], U8)
        AR2 = sb("AR2", [P, 32768], U8)
        AR3 = sb("AR3", [P, 16384], U8)

        def view(ar, off, shape, dt):
            v = ar[:shape[0], off:off + int(np.prod(shape[1:])) * mybir.dt.size(dt)].bitcast(dt)
            if len(shape) == 3:
                v = v.rearrange("p (a b) -> p a b", a=shape[1])
            return v

        xload = [view(AR1, 0, [P, D], F32), view(AR1, 16384, [P, D], F32)]
        junkb = view(AR3, 0, [P, D], BF16)
        xn_buf = view(AR3, 8192, [P, D], BF16)
        wqc = [view(AR2, b * 8192, [P, 2048], F32) for b in range(4)]
        aT_f = view(AR1, 0, [P, TOK], F32)
        kT_f = view(AR1, 4096, [P, TOK], F32)
        vT_f = view(AR1, 8192, [P, TOK], F32)
        Ebuf = [view(AR3, 0, [P, TOK], BF16), view(AR3, 2048, [P, TOK], BF16)]
        numZ = view(AR3, 4096, [P, 2048], F32)
        afm = [AR1[0:1, 12288 + b * 4096:12288 + (b + 1) * 4096].bitcast(BF16) for b in range(2)]
        kfm = [AR1[0:1, 20480 + b * 4096:20480 + (b + 1) * 4096].bitcast(BF16) for b in range(2)]
        xTb = [view(AR1, 0, [P, TOK], F32), view(AR1, 4096, [P, TOK], F32)]
        rest = [view(AR1, 8192, [P, TOK], F32), view(AR1, 12288, [P, TOK], F32)]
        outst = [view(AR1, 16384, [P, TOK], F32), view(AR1, 20480, [P, TOK], F32)]
        base1 = [view(AR1, 24576, [P, 512], F32), view(AR1, 26624, [P, 512], F32)]
        selb = view(AR1, 28672, [P, 512], F32)
        wch = [view(AR2, 0, [P, D], BF16), view(AR2, 8192, [P, D], BF16)]
        mtb = [[view(AR2, 24576 + (pl * 2 + b) * 1024, [P, TOK], U8) for b in range(2)]
               for pl in range(3)]
        bnst = sb("bnst", [P, 48], F32)
        xnb = [xn_buf, junkb]

        PS = ctx.enter_context(nc.psum_tensor("PS", [P, 4096], F32))[:]
        PSb = PS.bitcast(BF16)
        bank = lambda b: PS[:, 512 * b:512 * (b + 1)]
        bankb = lambda b: PSb[:, 1024 * b:1024 * (b + 1)]

        sems = {}
        for name in ["s_vp", "s_ap",
                     "s_xa0", "s_xa1", "s_wqc0", "s_wqc1", "s_wqc2", "s_wqc3",
                     "s_cst", "s_gcst", "s_tr", "s_sel", "s_cv",
                     "s_ptA", "s_qpe", "s_akvp",
                     "s_ad", "s_so", "s_ec", "s_afm0", "s_afm1",
                     "s_wql", "s_ccv", "s_wqg",
                     "s_wch0", "s_wch1", "s_mt0", "s_mt1", "s_xt0", "s_xt1",
                     "s_peb", "s_pey", "s_ptE", "s_out"]:
            sems[name] = ctx.enter_context(nc.semaphore(name))
        S = type("SS", (), sems)
        s_wqc = [S.s_wqc0, S.s_wqc1, S.s_wqc2, S.s_wqc3]
        s_xa = [S.s_xa0, S.s_xa1]
        s_afm = [S.s_afm0, S.s_afm1]
        s_wch = [S.s_wch0, S.s_wch1]
        s_mt = [S.s_mt0, S.s_mt1]
        s_xt = [S.s_xt0, S.s_xt1]

        marks = {}
        cT = lambda T, i: stats[:, 8 * T + i:8 * T + i + 1]

        def make_pass(counting):
            def mark(name, px):
                if counting:
                    marks[name] = px.n
                else:
                    assert marks[name] == px.n, (name, marks[name], px.n)

            def waitm(eng, name, sem):
                if not counting:
                    eng.wait_ge(sem, marks[name])

            def wq_tree(vector, k2):
                vector.wait_ge(s_wqc[k2 % 4], 16 * (k2 // 4 + 1))
                v32 = wqc[k2 % 4].rearrange("p (d h) -> p d h", d=64)
                k, h = k2 // 2, k2 % 2
                with nc.allow_low_precision(reason="head-sum output feeds bf16 matmul"):
                    vector.reduce_sum(wqs[:, k * 128 + h * 64:k * 128 + (h + 1) * 64],
                                      v32, axis=AX.X)
                mark(f"tree{k2}", vector)

            def xnT_evac(scalar, T):
                for g in range(4):
                    i = 4 * T + g
                    scalar.wait_ge(S.s_ptA, i + 1)
                    dst = xnT.rearrange("p (c t) -> p c t", c=NC)[:, g * 8:(g + 1) * 8,
                                                                 T * P:(T + 1) * P]
                    src = bankb(4 + g).rearrange("p (c t) -> p c t", c=8)
                    scalar.activation(dst, src, AF.Copy)
                    mark(f"xne{i}", scalar)

            def emit_vector(vector):
                for T in range(NT):
                    vector.wait_ge(s_xa[T % 2], 16 * (T // 2 + 1))
                    for g in range(8):
                        vector.bn_stats(bnst[:, 6 * g:6 * (g + 1)],
                                        xload[T % 2][:, 512 * g:512 * (g + 1)])
                    vector.wait_prev()
                    vector.bn_aggr(stats[:, 8 * T + 2:8 * T + 4], bnst)
                    mark(f"lnd{T}", vector)
                    waitm(vector, f"rst{T}", S.s_ap)
                    vector.wait_prev()
                    vector.scalar_tensor_tensor(cT(T, 6), cT(T, 2), -1.0, cT(T, 5),
                                                ALU.mult, ALU.mult)
                    mark(f"bia{T}", vector)
                    if T < 4:
                        wq_tree(vector, 2 * T)
                        wq_tree(vector, 2 * T + 1)
                vector.wait_ge(S.s_cst, 16 * 6)
                vector.reduce_sum(bqs[:, 1:2], bq_sb, axis=AX.X)
                vector.wait_prev()
                vector.tensor_scalar_mul(bqs[:, 0:1], bqs[:, 1:2], float(SCALE))
                mark("bqs", vector)
                vector.wait_ge(S.s_qpe, 3)
                vector.tensor_scalar_mul(hT, PS[:, 3072:4096], 8.0)
                mark("he", vector)
                for T in range(NT):
                    vector.wait_ge(S.s_akvp, T + 1)
                    vector.tensor_copy(a_tm[:, T * P:(T + 1) * P], PS[:, 3584:3712])
                    vector.tensor_copy(k_tm[:, T * P:(T + 1) * P], PS[:, 3712:3840])
                    mark(f"akve{T}", vector)
                vector.memset(v1, 1.0)
                vector.wait_prev()
                vector.tensor_copy(
                    v1.rearrange("p (t two) -> p t two", two=2)[:, :, 0:1],
                    vT_f[:, :, None])
                mark("v1m", vector)
                for b in range(4):
                    vector.wait_ge(S.s_ec, 32 * (b + 1))
                    vector.tensor_copy(numZ[:, b * 512:(b + 1) * 512], bank(4 + b))
                    nzv = numZ[:, b * 512:(b + 1) * 512].rearrange(
                        "p (t two) -> p t two", two=2)
                    vector.wait_prev()
                    vector.reciprocal(rzb, nzv[:, :, 1:2].rearrange("p t two -> p (t two)"))
                    vector.wait_prev()
                    vector.tensor_tensor(o_fm[:, b * 256:(b + 1) * 256],
                                         nzv[:, :, 0:1].rearrange("p t two -> p (t two)"),
                                         rzb, ALU.mult)
                    mark(f"ofm{b}", vector)
                for og in range(NC):
                    for th in range(2):
                        j = 2 * og + th
                        vector.wait_ge(S.s_pey, j + 1)
                        vector.wait_ge(s_mt[og % 2], 48 * (og // 2 + 1))
                        vector.tensor_copy(selb, bank(2))
                        for pl in range(3):
                            vector.wait_prev()
                            vector.copy_predicated(
                                selb, mtb[pl][og % 2][:, th * 512:(th + 1) * 512],
                                bank(3 + pl))
                        mark(f"sel{j}", vector)
                        waitm(vector, f"b1{j}", S.s_ap)
                        vector.wait_ge(s_xt[og % 2], 16 * (og // 2 + 1))
                        if og >= 2 and th == 0:
                            vector.wait_ge(S.s_ptE, 2 * (og - 1))
                        rst = rest[og % 2][:, th * 512:(th + 1) * 512]
                        vector.wait_prev()
                        vector.scalar_tensor_tensor(rst, base1[j % 2], 1.0, selb,
                                                    ALU.mult, ALU.add)
                        vector.wait_prev()
                        vector.scalar_tensor_tensor(
                            rst, xTb[og % 2][:, th * 512:(th + 1) * 512], 1.0, rst,
                            ALU.mult, ALU.add)
                        mark(f"cmb{j}", vector)

            def emit_scalar(scalar):
                for T in range(NT):
                    if T >= 1:
                        xnT_evac(scalar, T - 1)
                    if T == 0:
                        scalar.wait_ge(S.s_cv, 16)
                    waitm(scalar, f"lnd{T}", S.s_vp)
                    scalar.activation(cT(T, 4), cT(T, 3), AF.Ln, bias=cvec[:, 0:1])
                    scalar.wait_prev()
                    scalar.activation(cT(T, 5), cT(T, 4), AF.Exp, scale=cvec[:, 1:2])
                    mark(f"rst{T}", scalar)
                    waitm(scalar, f"bia{T}", S.s_vp)
                    if T >= 2:
                        scalar.wait_ge(S.s_ptA, 4 * (T - 1))
                    scalar.wait_prev()
                    scalar.activation(xnb[T % 2], xload[T % 2], AF.Identity,
                                      bias=cT(T, 6), scale=cT(T, 5))
                    mark(f"xn{T}", scalar)
                xnT_evac(scalar, NT - 1)
                scalar.wait_ge(S.s_qpe, 4)
                waitm(scalar, "bqs", S.s_vp)
                scalar.activation(aT_f, PS[:, 0:1024], AF.Identity,
                                  bias=bqs[:, 0:1], scale=cvec[:, 2:3])
                mark("qke1", scalar)
                scalar.wait_ge(S.s_qpe, 1)
                scalar.activation(kT_f, PS[:, 1024:2048], AF.Identity, bias=bkv[:, 0:1])
                mark("qke2", scalar)
                scalar.wait_ge(S.s_qpe, 2)
                scalar.activation(vT_f, PS[:, 2048:3072], AF.Identity, bias=bkv[:, 1:2])
                mark("qke3", scalar)
                for g in range(2 * NT * NQ8):
                    scalar.wait_ge(S.s_so, g + 1)
                    if g >= 2:
                        scalar.wait_ge(S.s_ec, g - 1)
                    scalar.activation(Ebuf[g % 2], PS[:, 1024 * (g % 2):1024 * (g % 2) + 1024],
                                      AF.Exp)
                    mark(f"Eg{g}", scalar)
                for og in range(NC):
                    for th in range(2):
                        j = 2 * og + th
                        scalar.wait_ge(S.s_peb, j + 1)
                        if j >= 2:
                            waitm(scalar, f"cmb{j - 2}", S.s_vp)
                        scalar.activation(base1[j % 2], bank(th),
                                          AF.Identity, bias=b0f[:, og:og + 1])
                        mark(f"b1{j}", scalar)
                    if og >= 1:
                        scalar.wait_ge(S.s_ptE, 2 * og)
                        if og >= 2:
                            scalar.wait_ge(S.s_out, 16 * (og - 1))
                        scalar.activation(outst[(og - 1) % 2][:, 0:512], bank(6), AF.Copy)
                        scalar.wait_prev()
                        scalar.activation(outst[(og - 1) % 2][:, 512:1024], bank(7), AF.Copy)
                        mark(f"ost{og - 1}", scalar)
                scalar.wait_ge(S.s_ptE, 2 * NC)
                scalar.wait_ge(S.s_out, 16 * (NC - 1))
                scalar.activation(outst[(NC - 1) % 2][:, 0:512], bank(6), AF.Copy)
                scalar.wait_prev()
                scalar.activation(outst[(NC - 1) % 2][:, 512:1024], bank(7), AF.Copy)
                mark(f"ost{NC - 1}", scalar)

            def emit_sync(sync):
                sync.dma_start(out=cvec, in_=cvec_in).then_inc(S.s_cv, 16)
                sync.dma_start(out=xload[0],
                               in_=x_in[0:P, :]).then_inc(s_xa[0], 16)
                sync.dma_start(out=idf, in_=idf_in).then_inc(S.s_cst, 16)
                sync.dma_start(out=idb, in_=idb_in).then_inc(S.s_cst, 16)
                sync.wait_ge(s_xa[0], 16)
                sync.dma_start(out=xload[1],
                               in_=x_in[P:2 * P, :]).then_inc(s_xa[1], 16)
                sync.dma_start(out=bq_sb, in_=bq_in).then_inc(S.s_cst, 16)
                sync.dma_start(out=bkv[:, 0:1], in_=bk_in).then_inc(S.s_cst, 16)
                sync.dma_start(out=bkv[:, 1:2], in_=bv_in).then_inc(S.s_cst, 16)
                sync.dma_start(out=b0f, in_=b0f_in).then_inc(S.s_cst, 16)
                def wq_half(k2):
                    k, h = k2 // 2, k2 % 2
                    if k2 >= 4:
                        waitm(sync, f"tree{k2 - 4}", S.s_vp)
                    sync.dma_start(
                        out=wqc[k2 % 4],
                        in_=wq_in[k * P:(k + 1) * P, h * 2048:(h + 1) * 2048],
                    ).then_inc(s_wqc[k2 % 4], 16)
                for k2 in range(2):
                    wq_half(k2)
                sync.dma_start(out=selt, in_=selt_in).then_inc(S.s_sel, 16)
                for k2 in range(2, 4):
                    wq_half(k2)
                for k in range(2, NT):
                    waitm(sync, f"xn{k - 2}", S.s_ap)
                    sync.dma_start(out=xload[k % 2],
                                   in_=x_in[k * P:(k + 1) * P, :]).then_inc(s_xa[k % 2], 16)
                    if k in (2, 3):
                        wq_half(2 * k)
                        wq_half(2 * k + 1)
                sync.wait_ge(S.s_ccv, 1)
                sync.dma_start(
                    out=wqs.rearrange("p (r j d) -> p r j d", r=8, j=4),
                    in_=wqsg_d.rearrange("(r p) (j d) -> p r j d", p=P, j=4),
                ).then_inc(S.s_wqg, 16)
                for c32 in range(64):
                    T = c32 // 8
                    sync.wait_ge(S.s_ad, 32 * (T + 1))
                    if c32 >= 2:
                        sync.wait_ge(S.s_so, 2 * (c32 - 1))
                    off = (c32 % 8) * 2048
                    sync.dma_start(out=afm[c32 % 2],
                                   in_=abounce[T].rearrange("t d -> (t d)")[off:off + 2048][None, :],
                                   ).then_inc(s_afm[c32 % 2], 16)
                    sync.dma_start(out=kfm[c32 % 2],
                                   in_=kbounce[T].rearrange("t d -> (t d)")[off:off + 2048][None, :],
                                   ).then_inc(s_afm[c32 % 2], 16)
                for og in range(NC):
                    if og == 0:
                        waitm(sync, "tree7", S.s_vp)
                    if og >= 2:
                        waitm(sync, f"sel{2 * (og - 2) + 1}", S.s_vp)
                    for pl in range(3):
                        sync.dma_start(out=mtb[pl][og % 2],
                                       in_=mt_in[pl][og * P:(og + 1) * P, :]).then_inc(s_mt[og % 2], 16)
                    if og == 0:
                        sync.wait_ge(S.s_akvp, NT)
                        waitm(sync, "v1m", S.s_vp)
                    if og >= 2:
                        waitm(sync, f"cmb{2 * (og - 2) + 1}", S.s_vp)
                    sync.dma_start(out=xTb[og % 2],
                                   in_=xT_in[og * P:(og + 1) * P, :]).then_inc(s_xt[og % 2], 16)
                    if og >= 1:
                        waitm(sync, f"ost{og - 1}", S.s_ap)
                        sync.dma_start(
                            out=out_d[:, (og - 1) * P:og * P].rearrange("(T t) j -> t T j", t=P),
                            in_=outst[(og - 1) % 2].rearrange("p (T j) -> p T j", T=NT),
                        ).then_inc(S.s_out, 16)
                waitm(sync, f"ost{NC - 1}", S.s_ap)
                sync.dma_start(
                    out=out_d[:, (NC - 1) * P:NC * P].rearrange("(T t) j -> t T j", t=P),
                    in_=outst[(NC - 1) % 2].rearrange("p (T j) -> p T j", T=NT),
                ).then_inc(S.s_out, 16)
                sync.wait_ge(S.s_out, 16 * NC)

            def emit_gpsimd(gpsimd):
                waitm(gpsimd, "tree7", S.s_vp)
                gpsimd.dma_start(out=wqsl_d, in_=wqs[:, 0:512]).then_inc(S.s_wql, 16)
                gpsimd.wait_ge(S.s_wql, 16)
                gpsimd.collective_compute(
                    "AllGather", ALU.bypass,
                    replica_groups=[list(range(8))],
                    ins=[wqsl_d], outs=[wqsg_d]).then_inc(S.s_ccv)
                gpsimd.dma_start(out=wk, in_=wk_in.rearrange("(c p) d -> p c d", p=P),
                                 ).then_inc(S.s_gcst, 16)
                gpsimd.dma_start(out=wv, in_=wv_in.rearrange("(c p) d -> p c d", p=P),
                                 ).then_inc(S.s_gcst, 16)
                gpsimd.dma_start(out=at_pad, in_=at_in.rearrange("(c p) q -> p c q", p=P),
                                 ).then_inc(S.s_gcst, 16)
                gpsimd.dma_start(out=bt_pad, in_=bt_in).then_inc(S.s_gcst, 16)
                for T in range(NT):
                    waitm(gpsimd, f"akve{T}", S.s_vp)
                    if T >= 1:
                        gpsimd.wait_ge(S.s_ad, 32 * T)
                    gpsimd.dma_start(out=abounce[T],
                                     in_=a_tm[:, T * P:(T + 1) * P]).then_inc(S.s_ad, 16)
                    gpsimd.dma_start(out=kbounce[T],
                                     in_=k_tm[:, T * P:(T + 1) * P]).then_inc(S.s_ad, 16)
                for og in range(NC):
                    if og == 0:
                        waitm(gpsimd, "tree7", S.s_vp)
                    if og >= 2:
                        gpsimd.wait_ge(S.s_peb, 2 * (og - 1))
                    gpsimd.dma_start(out=wch[og % 2], in_=w0th_in[og]).then_inc(s_wch[og % 2], 16)

            def emit_tensor(tensor):
                tensor.wait_ge(S.s_cst, 16 * 6)
                for T in range(NT):
                    waitm(tensor, f"xn{T}", S.s_ap)
                    for g in range(4):
                        for cc in range(8):
                            c = g * 8 + cc
                            ins = tensor.transpose(bankb(4 + g)[:, cc * 128:(cc + 1) * 128],
                                                   xnb[T % 2][:, c * 128:(c + 1) * 128], idb)
                            if cc == 7:
                                ins.then_inc(S.s_ptA, 1)
                tensor.wait_ge(S.s_gcst, 16 * 4)
                for th in range(2):
                    if th == 0:
                        waitm(tensor, "xne15", S.s_ap)
                    else:
                        waitm(tensor, f"xne{4 * NT - 1}", S.s_ap)
                    for wi, wten in [(1, wk), (2, wv), (3, at_pad), (0, wqs)]:
                        if th == 0 and wi == 2:
                            waitm(tensor, "xne28", S.s_ap)
                        if th == 0 and wi == 3:
                            waitm(tensor, "xne30", S.s_ap)
                        if th == 0 and wi == 0:
                            tensor.wait_ge(S.s_wqg, 16)
                        for c in range(NC):
                            ins = tensor.matmul(
                                PS[:, wi * 1024 + th * 512: wi * 1024 + (th + 1) * 512],
                                wten[:, c * 128:(c + 1) * 128],
                                xnT[:, c * TOK + th * 512:c * TOK + (th + 1) * 512],
                                start=(c == 0), stop=(c == NC - 1))
                            if c == NC - 1 and th == 1:
                                ins.then_inc(S.s_qpe, 1)
                tensor.wait_ge(S.s_qpe, 4)
                waitm(tensor, "qke3", S.s_ap)
                waitm(tensor, "he", S.s_vp)
                for T in range(NT):
                    if T >= 1:
                        waitm(tensor, f"akve{T - 1}", S.s_vp)
                    tensor.transpose(PS[:, 3584:3712], aT_f[:, T * P:(T + 1) * P], idf)
                    tensor.transpose(PS[:, 3712:3840], kT_f[:, T * P:(T + 1) * P],
                                     idf).then_inc(S.s_akvp, 1)
                tensor.wait_ge(S.s_akvp, NT)
                waitm(tensor, f"akve{NT - 1}", S.s_vp)
                waitm(tensor, "v1m", S.s_vp)
                NG = 2 * NT * NQ8
                for g in range(NG + 1):
                    if g < NG:
                        c32 = g // 2
                        if g >= 2:
                            waitm(tensor, f"Eg{g - 2}", S.s_ap)
                        tensor.wait_ge(s_afm[c32 % 2], 32 * (c32 // 2 + 1))
                        for j in range(8):
                            t = 8 * g + j
                            tl = t % 16
                            ins = tensor.matmul(
                                PS[:, 1024 * (g % 2) + 128 * j:1024 * (g % 2) + 128 * (j + 1)],
                                kfm[c32 % 2][0:1, tl * 128:(tl + 1) * 128],
                                afm[c32 % 2][0:1, tl * 128:(tl + 1) * 128],
                                start=True, stop=True)
                            if j == 7:
                                ins.then_inc(S.s_so, 1)
                    if g >= 1:
                        gg = g - 1
                        waitm(tensor, f"Eg{gg}", S.s_ap)
                        for j in range(8):
                            t = 8 * gg + j
                            bnk = 4 + t // 256
                            ins = tensor.matmul(
                                PS[:, bnk * 512 + 2 * (t % 256):bnk * 512 + 2 * (t % 256) + 2],
                                Ebuf[gg % 2][:, 128 * j:128 * (j + 1)],
                                v1[:, 2 * t:2 * t + 2],
                                start=True, stop=True)
                            if j == 7:
                                ins.then_inc(S.s_ec, 1)
                waitm(tensor, "ofm3", S.s_vp)
                tensor.wait_ge(S.s_sel, 16)
                for og in range(NC):
                    tensor.wait_ge(s_wch[og % 2], 16 * (og // 2 + 1))
                    for th in range(2):
                        j = 2 * og + th
                        if og >= 1:
                            waitm(tensor, f"b1{2 * (og - 1) + th}", S.s_ap)
                        for c in range(NC):
                            tensor.matmul(bank(th),
                                          wch[og % 2][:, c * 128:(c + 1) * 128],
                                          xnT[:, c * TOK + th * 512:c * TOK + (th + 1) * 512],
                                          start=(c == 0), stop=False)
                        tensor.matmul(bank(th), selt[:, og * P:(og + 1) * P],
                                      o_fm[:, th * 512:(th + 1) * 512],
                                      start=False, stop=True).then_inc(S.s_peb, 1)
                        if j >= 1:
                            waitm(tensor, f"sel{j - 1}", S.s_vp)
                        for l in range(4):
                            ins = tensor.matmul(
                                bank(2 + l),
                                bt_pad[32 * l:32 * (l + 1), og * P:(og + 1) * P],
                                hT[32 * l:32 * (l + 1), th * 512:(th + 1) * 512],
                                start=True, stop=True, tile_position=(32 * l, 0))
                            if l == 3:
                                ins.then_inc(S.s_pey, 1)
                    tensor.wait_ge(S.s_pey, 2 * og + 2)
                    if og >= 1:
                        waitm(tensor, f"cmb{2 * (og - 1) + 1}", S.s_vp)
                        if og >= 2:
                            waitm(tensor, f"ost{og - 2}", S.s_ap)
                        for T in range(NT):
                            b = 6 + T // 4
                            ins = tensor.transpose(bank(b)[:, (T % 4) * 128:(T % 4 + 1) * 128],
                                                   rest[(og - 1) % 2][:, T * P:(T + 1) * P], idf)
                            if T == NT - 1:
                                ins.then_inc(S.s_ptE, 2)
                waitm(tensor, f"cmb{2 * NC - 1}", S.s_vp)
                waitm(tensor, f"ost{NC - 2}", S.s_ap)
                for T in range(NT):
                    b = 6 + T // 4
                    ins = tensor.transpose(bank(b)[:, (T % 4) * 128:(T % 4 + 1) * 128],
                                           rest[(NC - 1) % 2][:, T * P:(T + 1) * P], idf)
                    if T == NT - 1:
                        ins.then_inc(S.s_ptE, 2)

            return emit_vector, emit_scalar, emit_sync, emit_gpsimd, emit_tensor

        ev, es, _, _, _ = make_pass(counting=True)
        ev(Prox(_Dry(), None, _V_OPS))
        es(Prox(_Dry(), None, _A_OPS))

        ev, es, esync, egp, etens = make_pass(counting=False)
        block = ctx.enter_context(nc.Block())

        @block.sync
        def _(sync):
            esync(sync)

        @block.gpsimd
        def _(gpsimd):
            egp(gpsimd)

        @block.scalar
        def _(scalar):
            es(Prox(scalar, S.s_ap, _A_OPS))

        @block.vector
        def _(vector):
            ev(Prox(vector, S.s_vp, _V_OPS))

        @block.tensor
        def _(tensor):
            etens(tensor)

    return nc


def _host_prep(L0_attn_x, L0_attn_Wq, L0_attn_bq, L0_attn_Wk, L0_attn_bk,
               L0_attn_Wv, L0_attn_bv, L0_ffn_W0, L0_ffn_b0, L0_ffn_A, L0_ffn_B):
    f32 = np.float32
    x = np.ascontiguousarray(np.asarray(L0_attn_x, f32).reshape(8192, D))
    wq_full = np.asarray(L0_attn_Wq, f32).reshape(D, D)
    bq = np.ascontiguousarray(np.asarray(L0_attn_bq, f32).reshape(DQ, NH))
    wk = np.ascontiguousarray(np.asarray(L0_attn_Wk, f32).reshape(D, DQ))
    bk = np.ascontiguousarray(np.asarray(L0_attn_bk, f32).reshape(DQ, 1))
    wv = np.ascontiguousarray(np.asarray(L0_attn_Wv, f32).reshape(D, DQ))
    bv = np.ascontiguousarray(np.asarray(L0_attn_bv, f32).reshape(DQ, 1))
    w0 = np.asarray(L0_ffn_W0, f32)
    b0 = np.asarray(L0_ffn_b0, f32)
    A = np.asarray(L0_ffn_A, f32)
    B = np.asarray(L0_ffn_B, f32)

    w0th = np.ascontiguousarray(
        w0.reshape(NC, P, NC, P).transpose(0, 3, 2, 1).reshape(NC, P, D))
    b0f = np.ascontiguousarray(b0.reshape(NC, P).T)
    at = np.zeros((D, P), dtype=f32)
    for l in range(4):
        at[:, 32 * l:32 * l + 16] = A[l].T
    bt = np.zeros((P, D), dtype=f32)
    for l in range(4):
        bt[32 * l:32 * l + 16, :] = B[l].T

    import jax
    with jax.default_device(jax.devices("cpu")[0]):
        u = np.asarray(jax.random.uniform(
            jax.random.key(42, impl="threefry2x32"), (4, 2048, D)))
    lidx = ((u >= 0.25).astype(np.int8) + (u >= 0.5) + (u >= 0.75)).reshape(8192, D)

    selt = np.zeros((P, D), dtype=ml_dtypes.bfloat16)
    selt[np.arange(D) // NH, np.arange(D)] = 1.0
    idf = np.eye(P, dtype=f32)
    idb = np.eye(P, dtype=ml_dtypes.bfloat16)
    cvec = np.zeros((P, 8), dtype=f32)
    cvec[:, 0] = EPS
    cvec[:, 1] = -0.5
    cvec[:, 2] = SCALE

    in_maps = []
    for c in range(8):
        sl = slice(c * TOK, (c + 1) * TOK)
        xs = x[sl]
        m = dict(x=np.ascontiguousarray(xs), xT=np.ascontiguousarray(xs.T),
                 wq=np.ascontiguousarray(wq_full[512 * c:512 * (c + 1), :]),
                 bq=bq, wk=wk, bk=bk, wv=wv, bv=bv,
                 w0th=w0th, b0f=b0f, at=at, bt=bt,
                 selt=selt, idf=idf, idb=idb, cvec=cvec)
        for l in (1, 2, 3):
            m[f"mt{l}"] = np.ascontiguousarray((lidx[sl] == l).astype(np.uint8).T)
        in_maps.append(m)
    return in_maps


_NC_CACHE = None


def kernel(**inputs):
    global _NC_CACHE
    in_maps = _host_prep(**inputs)
    if _NC_CACHE is None:
        _NC_CACHE = build_nc()
    res = run_bass_kernel_spmd(_NC_CACHE, in_maps, core_ids=list(range(8)))
    out = np.concatenate([r["out"] for r in res.results], axis=0)
    return out.reshape(4, 2048, D)


# version 26 (final)
# speedup vs baseline: 1.0798x; 1.0075x over previous
import numpy as np
import ml_dtypes
import concourse.bass as bass
import concourse.mybir as mybir
from concourse.bass_utils import run_bass_kernel_spmd
from contextlib import ExitStack

F32 = mybir.dt.float32
BF16 = mybir.dt.bfloat16
U8 = mybir.dt.uint8
AF = mybir.ActivationFunctionType
ALU = mybir.AluOpType
AX = mybir.AxisListType

P = 128
D = 4096
NT = 8
TOK = NT * P
NC = D // P
NH = 32
DQ = 128
SCALE = 1.0 / float(np.sqrt(128.0))
EPS = 1e-5
NQ8 = 8
DS = DQ // NQ8

_V_OPS = {"bn_stats", "bn_aggr",
          "tensor_tensor", "tensor_add", "tensor_sub", "tensor_mul", "tensor_max",
          "tensor_copy", "tensor_scalar", "tensor_single_scalar", "tensor_scalar_mul",
          "tensor_scalar_add", "scalar_tensor_tensor", "tensor_reduce", "reduce_sum",
          "reduce_max", "reciprocal", "copy_predicated", "select", "memset", "memzero"}
_A_OPS = {"activation", "copy", "add", "mul", "square", "sqrt", "memzero"}


class _DryIns:
    def then_inc(self, *a, **k):
        return self


class _Dry:
    def __getattr__(self, name):
        def f(*a, **kw):
            return _DryIns()
        return f


class Prox:
    def __init__(self, eng, sem, ops):
        self._eng = eng
        self._sem = sem
        self._ops = ops
        self.n = 0

    def __getattr__(self, name):
        fn = getattr(self._eng, name)
        if name in self._ops:
            def wrapped(*a, **kw):
                ins = fn(*a, **kw)
                ins.then_inc(self._sem, 1)
                self.n += 1
                return ins
            return wrapped
        return fn

    def wait_prev(self):
        if self.n > 0:
            self._eng.wait_ge(self._sem, self.n)


def build_nc():
    nc = bass.Bass("TRN2", target_bir_lowering=False)

    x_in = nc.dram_tensor("x", [TOK, D], F32, kind="ExternalInput").ap()
    xT_in = nc.dram_tensor("xT", [D, TOK], F32, kind="ExternalInput").ap()
    wq_in = nc.dram_tensor("wq", [512, D], F32, kind="ExternalInput").ap()
    wqsl_d = nc.dram_tensor("wqsl", [P, 512], BF16, kind="Internal").ap()
    wqsg_d = nc.dram_tensor("wqsg", [8 * P, 512], BF16, kind="Internal").ap()
    bq_in = nc.dram_tensor("bq", [DQ, NH], F32, kind="ExternalInput").ap()
    wk_in = nc.dram_tensor("wk", [D, DQ], F32, kind="ExternalInput").ap()
    bk_in = nc.dram_tensor("bk", [DQ, 1], F32, kind="ExternalInput").ap()
    wv_in = nc.dram_tensor("wv", [D, DQ], F32, kind="ExternalInput").ap()
    bv_in = nc.dram_tensor("bv", [DQ, 1], F32, kind="ExternalInput").ap()
    w0th_in = nc.dram_tensor("w0th", [NC, P, D], F32, kind="ExternalInput").ap()
    b0f_in = nc.dram_tensor("b0f", [P, NC], F32, kind="ExternalInput").ap()
    at_in = nc.dram_tensor("at", [D, P], F32, kind="ExternalInput").ap()
    bt_in = nc.dram_tensor("bt", [P, D], F32, kind="ExternalInput").ap()
    mt_in = [nc.dram_tensor(f"mt{l}", [D, TOK], U8, kind="ExternalInput").ap() for l in (1, 2, 3)]
    selt_in = nc.dram_tensor("selt", [P, D], BF16, kind="ExternalInput").ap()
    idf_in = nc.dram_tensor("idf", [P, P], F32, kind="ExternalInput").ap()
    idb_in = nc.dram_tensor("idb", [P, P], BF16, kind="ExternalInput").ap()
    cvec_in = nc.dram_tensor("cvec", [P, 8], F32, kind="ExternalInput").ap()
    out_d = nc.dram_tensor("out", [TOK, D], F32, kind="ExternalOutput").ap()
    abounce = nc.dram_tensor("abounce", [NT, P, P], BF16, kind="Internal").ap()
    kbounce = nc.dram_tensor("kbounce", [NT, P, P], BF16, kind="Internal").ap()

    with ExitStack() as ctx:
        def sb(name, shape, dt):
            return ctx.enter_context(nc.sbuf_tensor(name, shape, dt))[:]

        idf = sb("idf_s", [P, P], F32)
        idb = sb("idb_s", [P, P], BF16)
        selt = sb("selt_s", [P, D], BF16)
        wqs = sb("wqs", [P, D], BF16)
        wk = sb("wk_s", [P, D], BF16)
        wv = sb("wv_s", [P, D], BF16)
        at_pad = sb("at_pad", [P, D], BF16)
        bt_pad = sb("bt_pad", [P, D], BF16)
        b0f = sb("b0f_s", [P, NC], F32)
        bq_sb = sb("bq_sb", [DQ, NH], F32)
        bkv = sb("bkv", [DQ, 2], F32)
        stats = sb("stats", [P, 64], F32)
        bqs = sb("bqs_c", [P, 2], F32)
        cvec = sb("cvec_s", [P, 8], F32)
        xnT = sb("xnT", [P, NC * TOK], BF16)
        hT = sb("hT", [P, TOK], BF16)
        o_fm = sb("o_fm", [P, TOK], BF16)
        a_tm = sb("a_tm", [P, TOK], BF16)
        k_tm = sb("k_tm", [P, TOK], BF16)
        v1 = sb("v1", [P, 2 * TOK], BF16)
        rzb = sb("rzb", [P, 256], F32)

        AR1 = sb("AR1", [P, 32768], U8)
        AR2 = sb("AR2", [P, 32768], U8)
        AR3 = sb("AR3", [P, 16384], U8)

        def view(ar, off, shape, dt):
            v = ar[:shape[0], off:off + int(np.prod(shape[1:])) * mybir.dt.size(dt)].bitcast(dt)
            if len(shape) == 3:
                v = v.rearrange("p (a b) -> p a b", a=shape[1])
            return v

        xload = [view(AR1, 0, [P, D], F32), view(AR1, 16384, [P, D], F32)]
        junkb = view(AR3, 0, [P, D], BF16)
        xn_buf = view(AR3, 8192, [P, D], BF16)
        wqc = [view(AR2, b * 8192, [P, 2048], F32) for b in range(4)]
        aT_f = view(AR1, 0, [P, TOK], F32)
        kT_f = view(AR1, 4096, [P, TOK], F32)
        vT_f = view(AR1, 8192, [P, TOK], F32)
        Ebuf = [view(AR3, 0, [P, TOK], BF16), view(AR3, 2048, [P, TOK], BF16)]
        numZ = view(AR3, 4096, [P, 2048], F32)
        afm = [AR1[0:1, 12288 + b * 4096:12288 + (b + 1) * 4096].bitcast(BF16) for b in range(2)]
        kfm = [AR1[0:1, 20480 + b * 4096:20480 + (b + 1) * 4096].bitcast(BF16) for b in range(2)]
        xTb = [view(AR1, 0, [P, TOK], F32), view(AR1, 4096, [P, TOK], F32)]
        rest = [view(AR1, 8192, [P, TOK], F32), view(AR1, 12288, [P, TOK], F32)]
        outst = [view(AR1, 16384, [P, TOK], F32), view(AR1, 20480, [P, TOK], F32)]
        base1 = [view(AR1, 24576, [P, 512], F32), view(AR1, 26624, [P, 512], F32)]
        selb = view(AR1, 28672, [P, 512], F32)
        wch = [view(AR2, 0, [P, D], BF16), view(AR2, 8192, [P, D], BF16)]
        mtb = [[view(AR2, 24576 + (pl * 2 + b) * 1024, [P, TOK], U8) for b in range(2)]
               for pl in range(3)]
        bnst = sb("bnst", [P, 48], F32)
        xnb = [xn_buf, junkb]

        PS = ctx.enter_context(nc.psum_tensor("PS", [P, 4096], F32))[:]
        PSb = PS.bitcast(BF16)
        bank = lambda b: PS[:, 512 * b:512 * (b + 1)]
        bankb = lambda b: PSb[:, 1024 * b:1024 * (b + 1)]

        sems = {}
        for name in ["s_vp", "s_ap",
                     "s_xa0", "s_xa1", "s_wqc0", "s_wqc1", "s_wqc2", "s_wqc3",
                     "s_cst", "s_gcst", "s_tr", "s_sel", "s_cv",
                     "s_ptA", "s_qpe", "s_akvp",
                     "s_ad", "s_so", "s_ec", "s_afm0", "s_afm1",
                     "s_wql", "s_ccv", "s_wqg",
                     "s_wch0", "s_wch1", "s_mt0", "s_mt1", "s_xt0", "s_xt1",
                     "s_peb", "s_pey", "s_ptE", "s_out"]:
            sems[name] = ctx.enter_context(nc.semaphore(name))
        S = type("SS", (), sems)
        s_wqc = [S.s_wqc0, S.s_wqc1, S.s_wqc2, S.s_wqc3]
        s_xa = [S.s_xa0, S.s_xa1]
        s_afm = [S.s_afm0, S.s_afm1]
        s_wch = [S.s_wch0, S.s_wch1]
        s_mt = [S.s_mt0, S.s_mt1]
        s_xt = [S.s_xt0, S.s_xt1]

        marks = {}
        cT = lambda T, i: stats[:, 8 * T + i:8 * T + i + 1]

        def make_pass(counting):
            def mark(name, px):
                if counting:
                    marks[name] = px.n
                else:
                    assert marks[name] == px.n, (name, marks[name], px.n)

            def waitm(eng, name, sem):
                if not counting:
                    eng.wait_ge(sem, marks[name])

            def wq_tree(vector, k2):
                vector.wait_ge(s_wqc[k2 % 4], 16 * (k2 // 4 + 1))
                v32 = wqc[k2 % 4].rearrange("p (d h) -> p d h", d=64)
                k, h = k2 // 2, k2 % 2
                with nc.allow_low_precision(reason="head-sum output feeds bf16 matmul"):
                    vector.reduce_sum(wqs[:, k * 128 + h * 64:k * 128 + (h + 1) * 64],
                                      v32, axis=AX.X)
                mark(f"tree{k2}", vector)

            def xnT_evac(scalar, T):
                for g in range(4):
                    i = 4 * T + g
                    scalar.wait_ge(S.s_ptA, i + 1)
                    dst = xnT.rearrange("p (c t) -> p c t", c=NC)[:, g * 8:(g + 1) * 8,
                                                                 T * P:(T + 1) * P]
                    src = bankb(4 + g).rearrange("p (c t) -> p c t", c=8)
                    scalar.activation(dst, src, AF.Copy)
                    mark(f"xne{i}", scalar)

            def emit_vector(vector):
                for T in range(NT):
                    vector.wait_ge(s_xa[T % 2], 16 * (T // 2 + 1))
                    for g in range(8):
                        vector.bn_stats(bnst[:, 6 * g:6 * (g + 1)],
                                        xload[T % 2][:, 512 * g:512 * (g + 1)])
                    vector.wait_prev()
                    vector.bn_aggr(stats[:, 8 * T + 2:8 * T + 4], bnst)
                    mark(f"lnd{T}", vector)
                    waitm(vector, f"rst{T}", S.s_ap)
                    vector.wait_prev()
                    vector.scalar_tensor_tensor(cT(T, 6), cT(T, 2), -1.0, cT(T, 5),
                                                ALU.mult, ALU.mult)
                    mark(f"bia{T}", vector)
                    if T < 4:
                        wq_tree(vector, 2 * T)
                        wq_tree(vector, 2 * T + 1)
                vector.wait_ge(S.s_cst, 16 * 6)
                vector.reduce_sum(bqs[:, 1:2], bq_sb, axis=AX.X)
                vector.wait_prev()
                vector.tensor_scalar_mul(bqs[:, 0:1], bqs[:, 1:2], float(SCALE))
                mark("bqs", vector)
                vector.wait_ge(S.s_qpe, 3)
                vector.tensor_scalar_mul(hT, PS[:, 3072:4096], 8.0)
                mark("he", vector)
                for T in range(NT):
                    vector.wait_ge(S.s_akvp, T + 1)
                    vector.tensor_copy(a_tm[:, T * P:(T + 1) * P], PS[:, 3584:3712])
                    vector.tensor_copy(k_tm[:, T * P:(T + 1) * P], PS[:, 3712:3840])
                    mark(f"akve{T}", vector)
                vector.memset(v1, 1.0)
                vector.wait_prev()
                vector.tensor_copy(
                    v1.rearrange("p (t two) -> p t two", two=2)[:, :, 0:1],
                    vT_f[:, :, None])
                mark("v1m", vector)
                for b in range(4):
                    vector.wait_ge(S.s_ec, 32 * (b + 1))
                    vector.tensor_copy(numZ[:, b * 512:(b + 1) * 512], bank(4 + b))
                    nzv = numZ[:, b * 512:(b + 1) * 512].rearrange(
                        "p (t two) -> p t two", two=2)
                    vector.wait_prev()
                    vector.reciprocal(rzb, nzv[:, :, 1:2].rearrange("p t two -> p (t two)"))
                    vector.wait_prev()
                    vector.tensor_tensor(o_fm[:, b * 256:(b + 1) * 256],
                                         nzv[:, :, 0:1].rearrange("p t two -> p (t two)"),
                                         rzb, ALU.mult)
                    mark(f"ofm{b}", vector)
                for og in range(NC):
                    for th in range(2):
                        j = 2 * og + th
                        vector.wait_ge(S.s_pey, j + 1)
                        vector.wait_ge(s_mt[og % 2], 48 * (og // 2 + 1))
                        vector.tensor_copy(selb, bank(2))
                        for pl in range(3):
                            vector.wait_prev()
                            vector.copy_predicated(
                                selb, mtb[pl][og % 2][:, th * 512:(th + 1) * 512],
                                bank(3 + pl))
                        mark(f"sel{j}", vector)
                        waitm(vector, f"b1{j}", S.s_ap)
                        vector.wait_ge(s_xt[og % 2], 16 * (og // 2 + 1))
                        if og >= 2 and th == 0:
                            vector.wait_ge(S.s_ptE, 2 * (og - 1))
                        rst = rest[og % 2][:, th * 512:(th + 1) * 512]
                        vector.wait_prev()
                        vector.scalar_tensor_tensor(rst, base1[j % 2], 1.0, selb,
                                                    ALU.mult, ALU.add)
                        vector.wait_prev()
                        vector.scalar_tensor_tensor(
                            rst, xTb[og % 2][:, th * 512:(th + 1) * 512], 1.0, rst,
                            ALU.mult, ALU.add)
                        mark(f"cmb{j}", vector)

            def emit_scalar(scalar):
                for T in range(NT):
                    if T >= 1:
                        xnT_evac(scalar, T - 1)
                    if T == 0:
                        scalar.wait_ge(S.s_cv, 16)
                    waitm(scalar, f"lnd{T}", S.s_vp)
                    scalar.activation(cT(T, 4), cT(T, 3), AF.Ln, bias=cvec[:, 0:1])
                    scalar.wait_prev()
                    scalar.activation(cT(T, 5), cT(T, 4), AF.Exp, scale=cvec[:, 1:2])
                    mark(f"rst{T}", scalar)
                    waitm(scalar, f"bia{T}", S.s_vp)
                    if T >= 2:
                        scalar.wait_ge(S.s_ptA, 4 * (T - 1))
                    scalar.wait_prev()
                    scalar.activation(xnb[T % 2], xload[T % 2], AF.Identity,
                                      bias=cT(T, 6), scale=cT(T, 5))
                    mark(f"xn{T}", scalar)
                xnT_evac(scalar, NT - 1)
                scalar.wait_ge(S.s_qpe, 4)
                waitm(scalar, "bqs", S.s_vp)
                scalar.activation(aT_f, PS[:, 0:1024], AF.Identity,
                                  bias=bqs[:, 0:1], scale=cvec[:, 2:3])
                mark("qke1", scalar)
                scalar.wait_ge(S.s_qpe, 1)
                scalar.activation(kT_f, PS[:, 1024:2048], AF.Identity, bias=bkv[:, 0:1])
                mark("qke2", scalar)
                scalar.wait_ge(S.s_qpe, 2)
                scalar.activation(vT_f, PS[:, 2048:3072], AF.Identity, bias=bkv[:, 1:2])
                mark("qke3", scalar)
                for g in range(2 * NT * NQ8):
                    scalar.wait_ge(S.s_so, g + 1)
                    if g >= 2:
                        scalar.wait_ge(S.s_ec, g - 1)
                    scalar.activation(Ebuf[g % 2], PS[:, 1024 * (g % 2):1024 * (g % 2) + 1024],
                                      AF.Exp)
                    mark(f"Eg{g}", scalar)
                for og in range(NC):
                    for th in range(2):
                        j = 2 * og + th
                        scalar.wait_ge(S.s_peb, j + 1)
                        if j >= 2:
                            waitm(scalar, f"cmb{j - 2}", S.s_vp)
                        scalar.activation(base1[j % 2], bank(th),
                                          AF.Identity, bias=b0f[:, og:og + 1])
                        mark(f"b1{j}", scalar)
                    if og >= 1:
                        scalar.wait_ge(S.s_ptE, 2 * og)
                        if og >= 2:
                            scalar.wait_ge(S.s_out, 16 * (og - 1))
                        scalar.activation(outst[(og - 1) % 2][:, 0:512], bank(6), AF.Copy)
                        scalar.wait_prev()
                        scalar.activation(outst[(og - 1) % 2][:, 512:1024], bank(7), AF.Copy)
                        mark(f"ost{og - 1}", scalar)
                scalar.wait_ge(S.s_ptE, 2 * NC)
                scalar.wait_ge(S.s_out, 16 * (NC - 1))
                scalar.activation(outst[(NC - 1) % 2][:, 0:512], bank(6), AF.Copy)
                scalar.wait_prev()
                scalar.activation(outst[(NC - 1) % 2][:, 512:1024], bank(7), AF.Copy)
                mark(f"ost{NC - 1}", scalar)

            def emit_sync(sync):
                sync.dma_start(out=cvec, in_=cvec_in).then_inc(S.s_cv, 16)
                sync.dma_start(out=xload[0],
                               in_=x_in[0:P, :]).then_inc(s_xa[0], 16)
                sync.dma_start(out=idf, in_=idf_in).then_inc(S.s_cst, 16)
                sync.dma_start(out=idb, in_=idb_in).then_inc(S.s_cst, 16)
                sync.wait_ge(s_xa[0], 16)
                sync.dma_start(out=xload[1],
                               in_=x_in[P:2 * P, :]).then_inc(s_xa[1], 16)
                sync.dma_start(out=bq_sb, in_=bq_in).then_inc(S.s_cst, 16)
                sync.dma_start(out=bkv[:, 0:1], in_=bk_in).then_inc(S.s_cst, 16)
                sync.dma_start(out=bkv[:, 1:2], in_=bv_in).then_inc(S.s_cst, 16)
                sync.dma_start(out=b0f, in_=b0f_in).then_inc(S.s_cst, 16)
                def wq_half(k2):
                    k, h = k2 // 2, k2 % 2
                    if k2 >= 4:
                        waitm(sync, f"tree{k2 - 4}", S.s_vp)
                    sync.dma_start(
                        out=wqc[k2 % 4],
                        in_=wq_in[k * P:(k + 1) * P, h * 2048:(h + 1) * 2048],
                    ).then_inc(s_wqc[k2 % 4], 16)
                for k2 in range(2):
                    wq_half(k2)
                sync.dma_start(out=selt, in_=selt_in).then_inc(S.s_sel, 16)
                for k2 in range(2, 4):
                    wq_half(k2)
                for k in range(2, NT):
                    waitm(sync, f"xn{k - 2}", S.s_ap)
                    sync.dma_start(out=xload[k % 2],
                                   in_=x_in[k * P:(k + 1) * P, :]).then_inc(s_xa[k % 2], 16)
                    if k in (2, 3):
                        wq_half(2 * k)
                        wq_half(2 * k + 1)
                sync.wait_ge(S.s_ccv, 1)
                sync.dma_start(
                    out=wqs.rearrange("p (r j d) -> p r j d", r=8, j=4),
                    in_=wqsg_d.rearrange("(r p) (j d) -> p r j d", p=P, j=4),
                ).then_inc(S.s_wqg, 16)
                for c32 in range(64):
                    T = c32 // 8
                    sync.wait_ge(S.s_ad, 32 * (T + 1))
                    if c32 >= 2:
                        sync.wait_ge(S.s_so, 2 * (c32 - 1))
                    off = (c32 % 8) * 2048
                    sync.dma_start(out=afm[c32 % 2],
                                   in_=abounce[T].rearrange("t d -> (t d)")[off:off + 2048][None, :],
                                   ).then_inc(s_afm[c32 % 2], 16)
                    sync.dma_start(out=kfm[c32 % 2],
                                   in_=kbounce[T].rearrange("t d -> (t d)")[off:off + 2048][None, :],
                                   ).then_inc(s_afm[c32 % 2], 16)
                for og in range(NC):
                    if og == 0:
                        waitm(sync, "tree7", S.s_vp)
                    if og >= 2:
                        waitm(sync, f"sel{2 * (og - 2) + 1}", S.s_vp)
                    for pl in range(3):
                        sync.dma_start(out=mtb[pl][og % 2],
                                       in_=mt_in[pl][og * P:(og + 1) * P, :]).then_inc(s_mt[og % 2], 16)
                    if og == 0:
                        sync.wait_ge(S.s_akvp, NT)
                        waitm(sync, "v1m", S.s_vp)
                    if og >= 2:
                        waitm(sync, f"cmb{2 * (og - 2) + 1}", S.s_vp)
                    sync.dma_start(out=xTb[og % 2],
                                   in_=xT_in[og * P:(og + 1) * P, :]).then_inc(s_xt[og % 2], 16)
                    if og >= 1:
                        waitm(sync, f"ost{og - 1}", S.s_ap)
                        sync.dma_start(
                            out=out_d[:, (og - 1) * P:og * P].rearrange("(T t) j -> t T j", t=P),
                            in_=outst[(og - 1) % 2].rearrange("p (T j) -> p T j", T=NT),
                        ).then_inc(S.s_out, 16)
                waitm(sync, f"ost{NC - 1}", S.s_ap)
                sync.dma_start(
                    out=out_d[:, (NC - 1) * P:NC * P].rearrange("(T t) j -> t T j", t=P),
                    in_=outst[(NC - 1) % 2].rearrange("p (T j) -> p T j", T=NT),
                ).then_inc(S.s_out, 16)
                sync.wait_ge(S.s_out, 16 * NC)

            def emit_gpsimd(gpsimd):
                waitm(gpsimd, "tree7", S.s_vp)
                gpsimd.dma_start(out=wqsl_d, in_=wqs[:, 0:512]).then_inc(S.s_wql, 16)
                gpsimd.wait_ge(S.s_wql, 16)
                gpsimd.collective_compute(
                    "AllGather", ALU.bypass,
                    replica_groups=[list(range(8))],
                    ins=[wqsl_d], outs=[wqsg_d]).then_inc(S.s_ccv)
                gpsimd.dma_start(out=wk, in_=wk_in.rearrange("(c p) d -> p c d", p=P),
                                 ).then_inc(S.s_gcst, 16)
                gpsimd.dma_start(out=wv, in_=wv_in.rearrange("(c p) d -> p c d", p=P),
                                 ).then_inc(S.s_gcst, 16)
                gpsimd.dma_start(out=at_pad, in_=at_in.rearrange("(c p) q -> p c q", p=P),
                                 ).then_inc(S.s_gcst, 16)
                gpsimd.dma_start(out=bt_pad, in_=bt_in).then_inc(S.s_gcst, 16)
                for T in range(NT):
                    waitm(gpsimd, f"akve{T}", S.s_vp)
                    if T >= 1:
                        gpsimd.wait_ge(S.s_ad, 32 * T)
                    gpsimd.dma_start(out=abounce[T],
                                     in_=a_tm[:, T * P:(T + 1) * P]).then_inc(S.s_ad, 16)
                    gpsimd.dma_start(out=kbounce[T],
                                     in_=k_tm[:, T * P:(T + 1) * P]).then_inc(S.s_ad, 16)
                for og in range(NC):
                    if og == 0:
                        waitm(gpsimd, "tree7", S.s_vp)
                    if og >= 2:
                        gpsimd.wait_ge(S.s_peb, 2 * (og - 1))
                    gpsimd.dma_start(out=wch[og % 2], in_=w0th_in[og]).then_inc(s_wch[og % 2], 16)

            def emit_tensor(tensor):
                tensor.wait_ge(S.s_cst, 16 * 6)
                for T in range(NT):
                    waitm(tensor, f"xn{T}", S.s_ap)
                    for g in range(4):
                        for cc in range(8):
                            c = g * 8 + cc
                            ins = tensor.transpose(bankb(4 + g)[:, cc * 128:(cc + 1) * 128],
                                                   xnb[T % 2][:, c * 128:(c + 1) * 128], idb)
                            if cc == 7:
                                ins.then_inc(S.s_ptA, 1)
                for th in range(2):
                    if th == 0:
                        waitm(tensor, "xne15", S.s_ap)
                    else:
                        waitm(tensor, f"xne{4 * NT - 1}", S.s_ap)
                    for wi, wten in [(1, wk), (2, wv), (3, at_pad), (0, wqs)]:
                        if th == 0:
                            if wi == 1:
                                tensor.wait_ge(S.s_gcst, 16)
                            elif wi == 2:
                                tensor.wait_ge(S.s_gcst, 32)
                            elif wi == 3:
                                tensor.wait_ge(S.s_gcst, 48)
                        if th == 0 and wi == 2:
                            waitm(tensor, "xne28", S.s_ap)
                        if th == 0 and wi == 3:
                            waitm(tensor, "xne30", S.s_ap)
                        if th == 0 and wi == 0:
                            tensor.wait_ge(S.s_wqg, 16)
                        for c in range(NC):
                            ins = tensor.matmul(
                                PS[:, wi * 1024 + th * 512: wi * 1024 + (th + 1) * 512],
                                wten[:, c * 128:(c + 1) * 128],
                                xnT[:, c * TOK + th * 512:c * TOK + (th + 1) * 512],
                                start=(c == 0), stop=(c == NC - 1))
                            if c == NC - 1 and th == 1:
                                ins.then_inc(S.s_qpe, 1)
                tensor.wait_ge(S.s_qpe, 4)
                waitm(tensor, "qke3", S.s_ap)
                waitm(tensor, "he", S.s_vp)
                for T in range(NT):
                    if T >= 1:
                        waitm(tensor, f"akve{T - 1}", S.s_vp)
                    tensor.transpose(PS[:, 3584:3712], aT_f[:, T * P:(T + 1) * P], idf)
                    tensor.transpose(PS[:, 3712:3840], kT_f[:, T * P:(T + 1) * P],
                                     idf).then_inc(S.s_akvp, 1)
                tensor.wait_ge(S.s_akvp, NT)
                waitm(tensor, f"akve{NT - 1}", S.s_vp)
                waitm(tensor, "v1m", S.s_vp)
                NG = 2 * NT * NQ8
                for g in range(NG + 1):
                    if g < NG:
                        c32 = g // 2
                        if g >= 2:
                            waitm(tensor, f"Eg{g - 2}", S.s_ap)
                        tensor.wait_ge(s_afm[c32 % 2], 32 * (c32 // 2 + 1))
                        for j in range(8):
                            t = 8 * g + j
                            tl = t % 16
                            ins = tensor.matmul(
                                PS[:, 1024 * (g % 2) + 128 * j:1024 * (g % 2) + 128 * (j + 1)],
                                kfm[c32 % 2][0:1, tl * 128:(tl + 1) * 128],
                                afm[c32 % 2][0:1, tl * 128:(tl + 1) * 128],
                                start=True, stop=True)
                            if j == 7:
                                ins.then_inc(S.s_so, 1)
                    if g >= 1:
                        gg = g - 1
                        waitm(tensor, f"Eg{gg}", S.s_ap)
                        for j in range(8):
                            t = 8 * gg + j
                            bnk = 4 + t // 256
                            ins = tensor.matmul(
                                PS[:, bnk * 512 + 2 * (t % 256):bnk * 512 + 2 * (t % 256) + 2],
                                Ebuf[gg % 2][:, 128 * j:128 * (j + 1)],
                                v1[:, 2 * t:2 * t + 2],
                                start=True, stop=True)
                            if j == 7:
                                ins.then_inc(S.s_ec, 1)
                waitm(tensor, "ofm3", S.s_vp)
                tensor.wait_ge(S.s_sel, 16)
                tensor.wait_ge(S.s_gcst, 16 * 4)
                for og in range(NC):
                    tensor.wait_ge(s_wch[og % 2], 16 * (og // 2 + 1))
                    for th in range(2):
                        j = 2 * og + th
                        if og >= 1:
                            waitm(tensor, f"b1{2 * (og - 1) + th}", S.s_ap)
                        for c in range(NC):
                            tensor.matmul(bank(th),
                                          wch[og % 2][:, c * 128:(c + 1) * 128],
                                          xnT[:, c * TOK + th * 512:c * TOK + (th + 1) * 512],
                                          start=(c == 0), stop=False)
                        tensor.matmul(bank(th), selt[:, og * P:(og + 1) * P],
                                      o_fm[:, th * 512:(th + 1) * 512],
                                      start=False, stop=True).then_inc(S.s_peb, 1)
                        if j >= 1:
                            waitm(tensor, f"sel{j - 1}", S.s_vp)
                        for l in range(4):
                            ins = tensor.matmul(
                                bank(2 + l),
                                bt_pad[32 * l:32 * (l + 1), og * P:(og + 1) * P],
                                hT[32 * l:32 * (l + 1), th * 512:(th + 1) * 512],
                                start=True, stop=True, tile_position=(32 * l, 0))
                            if l == 3:
                                ins.then_inc(S.s_pey, 1)
                    tensor.wait_ge(S.s_pey, 2 * og + 2)
                    if og >= 1:
                        waitm(tensor, f"cmb{2 * (og - 1) + 1}", S.s_vp)
                        if og >= 2:
                            waitm(tensor, f"ost{og - 2}", S.s_ap)
                        for T in range(NT):
                            b = 6 + T // 4
                            ins = tensor.transpose(bank(b)[:, (T % 4) * 128:(T % 4 + 1) * 128],
                                                   rest[(og - 1) % 2][:, T * P:(T + 1) * P], idf)
                            if T == NT - 1:
                                ins.then_inc(S.s_ptE, 2)
                waitm(tensor, f"cmb{2 * NC - 1}", S.s_vp)
                waitm(tensor, f"ost{NC - 2}", S.s_ap)
                for T in range(NT):
                    b = 6 + T // 4
                    ins = tensor.transpose(bank(b)[:, (T % 4) * 128:(T % 4 + 1) * 128],
                                           rest[(NC - 1) % 2][:, T * P:(T + 1) * P], idf)
                    if T == NT - 1:
                        ins.then_inc(S.s_ptE, 2)

            return emit_vector, emit_scalar, emit_sync, emit_gpsimd, emit_tensor

        ev, es, _, _, _ = make_pass(counting=True)
        ev(Prox(_Dry(), None, _V_OPS))
        es(Prox(_Dry(), None, _A_OPS))

        ev, es, esync, egp, etens = make_pass(counting=False)
        block = ctx.enter_context(nc.Block())

        @block.sync
        def _(sync):
            esync(sync)

        @block.gpsimd
        def _(gpsimd):
            egp(gpsimd)

        @block.scalar
        def _(scalar):
            es(Prox(scalar, S.s_ap, _A_OPS))

        @block.vector
        def _(vector):
            ev(Prox(vector, S.s_vp, _V_OPS))

        @block.tensor
        def _(tensor):
            etens(tensor)

    return nc


def _host_prep(L0_attn_x, L0_attn_Wq, L0_attn_bq, L0_attn_Wk, L0_attn_bk,
               L0_attn_Wv, L0_attn_bv, L0_ffn_W0, L0_ffn_b0, L0_ffn_A, L0_ffn_B):
    f32 = np.float32
    x = np.ascontiguousarray(np.asarray(L0_attn_x, f32).reshape(8192, D))
    wq_full = np.asarray(L0_attn_Wq, f32).reshape(D, D)
    bq = np.ascontiguousarray(np.asarray(L0_attn_bq, f32).reshape(DQ, NH))
    wk = np.ascontiguousarray(np.asarray(L0_attn_Wk, f32).reshape(D, DQ))
    bk = np.ascontiguousarray(np.asarray(L0_attn_bk, f32).reshape(DQ, 1))
    wv = np.ascontiguousarray(np.asarray(L0_attn_Wv, f32).reshape(D, DQ))
    bv = np.ascontiguousarray(np.asarray(L0_attn_bv, f32).reshape(DQ, 1))
    w0 = np.asarray(L0_ffn_W0, f32)
    b0 = np.asarray(L0_ffn_b0, f32)
    A = np.asarray(L0_ffn_A, f32)
    B = np.asarray(L0_ffn_B, f32)

    w0th = np.ascontiguousarray(
        w0.reshape(NC, P, NC, P).transpose(0, 3, 2, 1).reshape(NC, P, D))
    b0f = np.ascontiguousarray(b0.reshape(NC, P).T)
    at = np.zeros((D, P), dtype=f32)
    for l in range(4):
        at[:, 32 * l:32 * l + 16] = A[l].T
    bt = np.zeros((P, D), dtype=f32)
    for l in range(4):
        bt[32 * l:32 * l + 16, :] = B[l].T

    import jax
    with jax.default_device(jax.devices("cpu")[0]):
        u = np.asarray(jax.random.uniform(
            jax.random.key(42, impl="threefry2x32"), (4, 2048, D)))
    lidx = ((u >= 0.25).astype(np.int8) + (u >= 0.5) + (u >= 0.75)).reshape(8192, D)

    selt = np.zeros((P, D), dtype=ml_dtypes.bfloat16)
    selt[np.arange(D) // NH, np.arange(D)] = 1.0
    idf = np.eye(P, dtype=f32)
    idb = np.eye(P, dtype=ml_dtypes.bfloat16)
    cvec = np.zeros((P, 8), dtype=f32)
    cvec[:, 0] = EPS
    cvec[:, 1] = -0.5
    cvec[:, 2] = SCALE

    in_maps = []
    for c in range(8):
        sl = slice(c * TOK, (c + 1) * TOK)
        xs = x[sl]
        m = dict(x=np.ascontiguousarray(xs), xT=np.ascontiguousarray(xs.T),
                 wq=np.ascontiguousarray(wq_full[512 * c:512 * (c + 1), :]),
                 bq=bq, wk=wk, bk=bk, wv=wv, bv=bv,
                 w0th=w0th, b0f=b0f, at=at, bt=bt,
                 selt=selt, idf=idf, idb=idb, cvec=cvec)
        for l in (1, 2, 3):
            m[f"mt{l}"] = np.ascontiguousarray((lidx[sl] == l).astype(np.uint8).T)
        in_maps.append(m)
    return in_maps


_NC_CACHE = None


def kernel(**inputs):
    global _NC_CACHE
    in_maps = _host_prep(**inputs)
    if _NC_CACHE is None:
        _NC_CACHE = build_nc()
    res = run_bass_kernel_spmd(_NC_CACHE, in_maps, core_ids=list(range(8)))
    out = np.concatenate([r["out"] for r in res.results], axis=0)
    return out.reshape(4, 2048, D)
